# version 11
# baseline (speedup 1.0000x reference)
import numpy as np
import ml_dtypes
import concourse.bass as bass
import concourse.mybir as mybir
from concourse.bass_utils import run_bass_kernel_spmd

F32 = mybir.dt.float32
BF16 = mybir.dt.bfloat16
I8 = mybir.dt.int8
AF = mybir.ActivationFunctionType
OP = mybir.AluOpType
AX = mybir.AxisListType
BF = ml_dtypes.bfloat16
f32 = np.float32

D = 2048
T = 2048
S = 8192
INTER = 8192
NDT = 16
NIC = 64
NOC = 16
NT = 4
TT = 512
NH_LOC = 4
HD = 128
NKT = S // 128
QSCALE = float(1.0 / np.sqrt(HD))

ENGS = ("sync", "tensor", "vector", "scalar", "gpsimd")

PHASES = ("conv", "ffn1", "qkv", "attn", "oproj", "ffn2")


class Sem:
    __slots__ = ("h", "n")

    def __init__(self, h=None):
        self.h = h
        self.n = 0

    def inc(self, inst, k=1):
        if self.h is not None:
            inst.then_inc(self.h, k)
        self.n += k
        return self.n


class EvDict(dict):
    strict = True

    def __missing__(self, key):
        if self.strict:
            raise KeyError(key)
        return 0


class _MockInst:
    __slots__ = ()

    def then_inc(self, sem, k=1):
        return self


class _MockReg:
    pass


class _MockEng:
    def wait_ge(self, sem, val):
        pass

    def partition_id(self):
        return 0

    def register(self, name=None):
        from contextlib import nullcontext
        return nullcontext(_MockReg())

    def __getattr__(self, name):
        def _op(*a, **k):
            return _MockInst()
        return _op


SEM_NAMES = (
    "c_ld",
    "vg_l0", "vg_l1", "vu_l0", "vu_l1", "vd_l0", "vd_l1",
    "vg_s0", "vg_s1", "vu_s0", "vu_s1", "vd_s0", "vd_s1",
    "f_x", "f_w0", "f_w1", "f_d0", "f_d1", "f_o0", "f_o1", "f_r0", "f_r1",
    "q_x0", "q_x1", "q_w0", "q_w1", "q_s0", "q_s1",
    "a_l0", "a_l1", "a_st", "o_ld", "o_x0", "o_x1", "o_s0", "o_s1",
    "cc1g0", "cc1g1", "cc1g2", "cc1g3",
    "cc2g0", "cc2g1", "cc2g2", "cc2g3", "o_t0", "o_t1", "o_g2",
    "e_mm", "e_ve", "e_act", "e_po",
)


def phase_const(S_, C, ev):
    def sync(eng):
        for src, dst in C["const_pairs"]:
            S_["c_ld"].inc(eng.dma_start(out=dst[:, :], in_=src), 16)

    def vector(eng):
        eng.wait_ge(S_["c_ld"].h, S_["c_ld"].n)
        S_["e_ve"].inc(eng.tensor_scalar_mul(
            out=C["sdh_sb"][:, :], in0=C["sdh_sb"][:, :], scalar1=0.5))

    return {"sync": sync, "vector": vector}


def phase_convert(S_, C, ev):
    wg8, wu8, wd8 = C["wg8"], C["wu8"], C["wd8"]
    wg_bf, wu_bf, wd_bf = C["wg_bf"], C["wu_bf"], C["wd_bf"]
    bg8, bgb = C["cv_g8"], C["cv_gb"]
    bu8, bub = C["cv_u8"], C["cv_ub"]
    bd8, bdb = C["cv_d8"], C["cv_db"]
    sve, spo = S_["e_ve"], S_["e_po"]

    def sync(eng):
        for ic in range(NIC):
            p = ic % 2
            if ic >= 2:
                eng.wait_ge(sve.h, ev[("cv_g", ic - 2)])
                eng.wait_ge(spo.h, ev[("cv_u", ic - 2)])
            S_[f"vg_l{p}"].inc(eng.dma_start(
                out=bg8[p][:, :], in_=wg8[ic * 128:(ic + 1) * 128, :]), 16)
            ev[("g_ld", ic)] = S_[f"vg_l{p}"].n
            S_[f"vu_l{p}"].inc(eng.dma_start(
                out=bu8[p][:, :], in_=wu8[ic * 128:(ic + 1) * 128, :]), 16)
            ev[("u_ld", ic)] = S_[f"vu_l{p}"].n
        for oc in range(NOC):
            p = oc % 2
            if oc >= 2:
                s = sve if (oc - 2) % 2 == 0 else spo
                eng.wait_ge(s.h, ev[("cv_d", oc - 2)])
            S_[f"vd_l{p}"].inc(eng.dma_start(
                out=bd8[p][:, :], in_=wd8[oc * 128:(oc + 1) * 128, :]), 16)
            ev[("d_ld", oc)] = S_[f"vd_l{p}"].n

    def vector(eng):
        for ic in range(NIC):
            p = ic % 2
            eng.wait_ge(S_[f"vg_l{p}"].h, ev[("g_ld", ic)])
            if ic >= 2:
                eng.wait_ge(S_[f"vg_s{p}"].h, ev[("g_st", ic - 2)])
            ev[("cv_g", ic)] = sve.inc(
                eng.tensor_copy(out=bgb[p][:, :], in_=bg8[p][:, :]))
        for oc in range(0, NOC, 2):
            p = 0
            eng.wait_ge(S_[f"vd_l{p}"].h, ev[("d_ld", oc)])
            if oc >= 2:
                eng.wait_ge(S_[f"vd_s{p}"].h, ev[("d_st", oc - 2)])
            ev[("cv_d", oc)] = sve.inc(
                eng.tensor_copy(out=bdb[p][:, :], in_=bd8[p][:, :]))

    def gpsimd(eng):
        for ic in range(NIC):
            p = ic % 2
            eng.wait_ge(S_[f"vu_l{p}"].h, ev[("u_ld", ic)])
            if ic >= 2:
                eng.wait_ge(S_[f"vu_s{p}"].h, ev[("u_st", ic - 2)])
            ev[("cv_u", ic)] = spo.inc(
                eng.tensor_copy(out=bub[p][:, :], in_=bu8[p][:, :]))
        for oc in range(1, NOC, 2):
            p = 1
            eng.wait_ge(S_[f"vd_l{p}"].h, ev[("d_ld", oc)])
            if oc >= 2:
                eng.wait_ge(S_[f"vd_s{p}"].h, ev[("d_st", oc - 2)])
            ev[("cv_d", oc)] = spo.inc(
                eng.tensor_copy(out=bdb[p][:, :], in_=bd8[p][:, :]))

    def scalar(eng):
        for ic in range(NIC):
            p = ic % 2
            eng.wait_ge(sve.h, ev[("cv_g", ic)])
            ev[("g_st", ic)] = S_[f"vg_s{p}"].inc(eng.dma_start(
                out=wg_bf[ic * 128:(ic + 1) * 128, :], in_=bgb[p][:, :]), 16)
            eng.wait_ge(spo.h, ev[("cv_u", ic)])
            ev[("u_st", ic)] = S_[f"vu_s{p}"].inc(eng.dma_start(
                out=wu_bf[ic * 128:(ic + 1) * 128, :], in_=bub[p][:, :]), 16)
        for oc in range(NOC):
            p = oc % 2
            s = sve if oc % 2 == 0 else spo
            eng.wait_ge(s.h, ev[("cv_d", oc)])
            ev[("d_st", oc)] = S_[f"vd_s{p}"].inc(eng.dma_start(
                out=wd_bf[oc * 128:(oc + 1) * 128, :], in_=bdb[p][:, :]), 16)

    return {"sync": sync, "vector": vector, "gpsimd": gpsimd, "scalar": scalar}


def phase_ffn(S_, C, ev, src_d, dst_d, tag, do_conv=False):
    ones_sb = C["ones_sb"]; sg_sb = C["sg_sb"]; su_sb = C["su_sb"]; sdh_sb = C["sdh_sb"]
    xf = C["ffn_xf"]; xnb = C["ffn_xn2"]; hb = C["ffn_h"]
    xrb = C["ffn_xr"]
    wgb = C["ffn_wg"]; wub = C["ffn_wu"]; wdb = C["ffn_wd"]
    gsb = C["ffn_g"]; sqb = C["ffn_sq"]; xbb = C["ffn_xb"]; lntmp = C["ffn_lnt"]
    m_sb = C["ffn_m"]; v_sb = C["ffn_v"]; a_sb = C["ffn_A"]; b_sb = C["ffn_B"]
    x1b = C["ffn_x1"]
    ps_sum = C["PS"][0]; ps_sq = C["PS"][1]
    ps_g = C["PS"][2]; ps_u = C["PS"][3]; ps_d = [C["PS"][4], C["PS"][5]]
    wg_bf = C["wg_bf"]; wu_bf = C["wu_bf"]; wd_bf = C["wd_bf"]

    if do_conv:
        wg8_in, wu8_in, wd8_in = C["wg8"], C["wu8"], C["wd8"]
        stg = wdb[0].bitcast(I8)
        stg_g8 = [stg[:, 0:2048], stg[:, 2048:4096]]
        stg_u8 = [stg[:, 4096:6144], stg[:, 6144:8192]]
        stg_d8 = [stg[:, 8192:12288], stg[:, 12288:16384]]
        cvd_out = [wdb[1][:, 0:4096], wdb[1][:, 4096:8192]]
    sx = S_["f_x"]
    srp = [S_["f_r0"], S_["f_r1"]]
    swp = [S_["f_w0"], S_["f_w1"]]
    sdp = [S_["f_d0"], S_["f_d1"]]
    smm, sve, sact = S_["e_mm"], S_["e_ve"], S_["e_act"]
    sop = [S_["f_o0"], S_["f_o1"]]

    def K(*a):
        return (tag,) + a

    def sync(eng):
        def xld(tt):
            if tt > 0:
                eng.wait_ge(sve.h, ev[K("xn", tt - 1)])
            srcx = src_d[:, tt * TT:(tt + 1) * TT].rearrange(
                "(dt p) t -> p dt t", p=128)
            dst3 = xf[:, :].rearrange("p (dt t) -> p dt t", dt=NDT)
            sx.inc(eng.dma_start(out=dst3, in_=srcx), 16)
            ev[K("xld", tt)] = sx.n

        def wld(tt):
            if tt == 0 and do_conv:
                for ic in range(NIC):
                    p = ic % 2
                    if ic >= 2:
                        eng.wait_ge(sve.h, ev[K("cvg", ic - 2)])
                        eng.wait_ge(S_["e_po"].h, ev[K("cvu", ic - 2)])
                    S_[f"vg_l{p}"].inc(eng.dma_start(
                        out=stg_g8[p][:, :],
                        in_=wg8_in[ic * 128:(ic + 1) * 128, :]), 16)
                    ev[K("g8ld", ic)] = S_[f"vg_l{p}"].n
                    S_[f"vu_l{p}"].inc(eng.dma_start(
                        out=stg_u8[p][:, :],
                        in_=wu8_in[ic * 128:(ic + 1) * 128, :]), 16)
                    ev[K("u8ld", ic)] = S_[f"vu_l{p}"].n
                    if ic % 2 == 1:
                        c = ic // 2
                        p2 = c % 2
                        if c >= 2:
                            eng.wait_ge(S_["e_act"].h, ev[K("cvd", c - 2)])
                        oc, half = c // 2, c % 2
                        S_[f"vd_l{p2}"].inc(eng.dma_start(
                            out=stg_d8[p2][:, :],
                            in_=wd8_in[oc * 128:(oc + 1) * 128,
                                       half * 4096:(half + 1) * 4096]), 16)
                        ev[K("d8ld", c)] = S_[f"vd_l{p2}"].n
                return
            for ic in range(NIC):
                g = tt * NIC + ic
                p = g % 2
                if g >= 2:
                    eng.wait_ge(smm.h, ev[K("guw", g - 2)])
                if tt == 1 and do_conv and ic < 2:
                    eng.wait_ge(S_[f"vg_s{p}"].h, ev[K("gst", NIC - 2 + p)])
                    eng.wait_ge(S_[f"vu_s{p}"].h, ev[K("ust", NIC - 2 + p)])
                swp[p].inc(eng.dma_start(
                    out=wgb[p][:, :], in_=wg_bf[ic * 128:(ic + 1) * 128, :]), 16)
                swp[p].inc(eng.dma_start(
                    out=wub[p][:, :], in_=wu_bf[ic * 128:(ic + 1) * 128, :]), 16)
                ev[K("wld", g)] = swp[p].n

        def dld(tt):
            for oc in range(NOC):
                g = tt * NOC + oc
                p = g % 2
                if tt == 0 and do_conv and oc < 2:
                    eng.wait_ge(sve.h, ev[K("cvg_last")])
                    eng.wait_ge(S_["e_po"].h, ev[K("cvu_last")])
                    eng.wait_ge(S_["vd_s0"].h, ev[K("dst_last0")])
                    eng.wait_ge(S_["vd_s1"].h, ev[K("dst_last1")])
                if g >= 2:
                    eng.wait_ge(smm.h, ev[K("dmm", g - 2)])
                sdp[p].inc(eng.dma_start(
                    out=wdb[p][:, :], in_=wd_bf[oc * 128:(oc + 1) * 128, :]), 16)
                ev[K("dld", g)] = sdp[p].n

        xld(0)
        for tt in range(NT):
            wld(tt)
            if tt + 1 < NT:
                xld(tt + 1)
            dld(tt)

    def tensor(eng):
        def stats(tt):
            for dt in range(NDT):
                p = dt % 2
                eng.wait_ge(sve.h, ev[K("cast", tt, dt)])
                if dt == 0 and tt > 0:
                    eng.wait_ge(sve.h, ev[K("stats", tt - 1)])
                eng.matmul(ps_sum[:, :], ones_sb[:, :], xbb[p][:, :],
                           start=(dt == 0), stop=(dt == NDT - 1))
                mmi = eng.matmul(ps_sq[:, :], ones_sb[:, :], sqb[p][:, :],
                                 start=(dt == 0), stop=(dt == NDT - 1))
                ev[K("statp", tt, dt)] = smm.inc(mmi)

        def gateup(tt):
            xn = xnb[tt % 2]
            for ic in range(NIC):
                g = tt * NIC + ic
                p = g % 2
                if tt == 0 and do_conv:
                    eng.wait_ge(sve.h, ev[K("cvg", ic)])
                    eng.wait_ge(S_["e_po"].h, ev[K("cvu", ic)])
                else:
                    eng.wait_ge(swp[p].h, ev[K("wld", g)])
                if ic == 0:
                    eng.wait_ge(sve.h, ev[K("xn", tt)])
                if g > 0:
                    eng.wait_ge(sact.h, ev[K("silu", g - 1)])
                for dt in range(NDT):
                    mmi = eng.matmul(ps_g[:, :], wgb[p][:, dt * 128:(dt + 1) * 128],
                                     xn[:, dt * TT:(dt + 1) * TT],
                                     start=(dt == 0), stop=(dt == NDT - 1))
                ev[K("gmm", g)] = smm.inc(mmi)
                if g > 0:
                    eng.wait_ge(sve.h, ev[K("h", g - 1)])
                for dt in range(NDT):
                    mmi = eng.matmul(ps_u[:, :], wub[p][:, dt * 128:(dt + 1) * 128],
                                     xn[:, dt * TT:(dt + 1) * TT],
                                     start=(dt == 0), stop=(dt == NDT - 1))
                ev[K("guw", g)] = smm.inc(mmi)

        def down(tt):
            for oc in range(NOC):
                g = tt * NOC + oc
                p = g % 2
                eng.wait_ge(sdp[p].h, ev[K("dld", g)])
                if oc == 0:
                    eng.wait_ge(sve.h, ev[K("h", tt * NIC + NIC - 1)])
                if g >= 2:
                    eng.wait_ge(sve.h, ev[K("x1ve", (g - 2) // NOC, (g - 2) % NOC)])
                for ic in range(NIC):
                    mmi = eng.matmul(ps_d[p][:, :],
                                     wdb[p][:, ic * 128:(ic + 1) * 128],
                                     hb[:, ic * TT:(ic + 1) * TT],
                                     start=(ic == 0), stop=(ic == NIC - 1))
                ev[K("dmm", g)] = smm.inc(mmi)

        stats(0)
        for tt in range(NT):
            gateup(tt)
            if tt + 1 < NT:
                stats(tt + 1)
            down(tt)

    def vector(eng):
        def ln(tt):
            xn = xnb[tt % 2]
            for dt in range(NDT):
                p = dt % 2
                eng.wait_ge(sx.h, ev[K("xld", tt)])
                if dt >= 2 or tt > 0:
                    e = (ev[K("statp", tt, dt - 2)] if dt >= 2 else
                         ev[K("statp", tt - 1, NDT - 2 + dt)])
                    eng.wait_ge(smm.h, e)
                xft = xf[:, dt * TT:(dt + 1) * TT]
                sve.inc(eng.tensor_copy(out=xbb[p][:, :], in_=xft))
                ev[K("cast", tt, dt)] = sve.inc(
                    eng.tensor_mul(out=sqb[p][:, :], in0=xft, in1=xft))
            eng.wait_ge(smm.h, ev[K("statp", tt, NDT - 1)])
            sve.inc(eng.tensor_scalar_mul(out=m_sb[:, :], in0=ps_sum[:, :],
                                          scalar1=float(1.0 / D)))
            eng.wait_ge(sve.h, sve.n)
            sve.inc(eng.scalar_tensor_tensor(out=v_sb[:, :], in0=m_sb[:, :],
                                             scalar=-1.0, in1=m_sb[:, :],
                                             op0=OP.mult, op1=OP.mult))
            eng.wait_ge(sve.h, sve.n)
            ev[K("stats", tt)] = sve.inc(
                eng.scalar_tensor_tensor(out=v_sb[:, :], in0=ps_sq[:, :],
                                         scalar=float(1.0 / D), in1=v_sb[:, :],
                                         op0=OP.mult, op1=OP.add))
            eng.wait_ge(sact.h, ev[K("rstd", tt)])
            sve.inc(eng.scalar_tensor_tensor(out=b_sb[:, :], in0=m_sb[:, :],
                                             scalar=-1.0, in1=a_sb[:, :],
                                             op0=OP.mult, op1=OP.mult))
            eng.wait_ge(sve.h, sve.n)
            for dt in range(NDT):
                xft = xf[:, dt * TT:(dt + 1) * TT]
                xnt = xn[:, dt * TT:(dt + 1) * TT]
                if tt >= 2 and dt == 0:
                    eng.wait_ge(smm.h, ev[K("guw", (tt - 2) * NIC + NIC - 1)])
                sve.inc(eng.tensor_mul(out=lntmp[:, :], in0=xft, in1=a_sb[:, :]))
                eng.wait_ge(sve.h, sve.n)
                sve.inc(eng.tensor_add(out=xnt, in0=lntmp[:, :], in1=b_sb[:, :]))
                eng.wait_ge(sve.h, sve.n)
            ev[K("xn", tt)] = sve.n

        def hp1(tt, ic):
            g = tt * NIC + ic
            eng.wait_ge(sact.h, ev[K("silu", g)])
            eng.wait_ge(smm.h, ev[K("guw", g)])
            ev[K("h", g)] = sve.inc(eng.scalar_tensor_tensor(
                out=hb[:, ic * TT:(ic + 1) * TT], in0=ps_u[:, :],
                scalar=su_sb[:, ic:ic + 1], in1=gsb[g % 2][:, :],
                op0=OP.mult, op1=OP.mult))

        def hpass(tt):
            for ic in range(NIC):
                hp1(tt, ic)

        def x1pass(tt):
            for oc in range(NOC):
                g = tt * NOC + oc
                p = g % 2
                eng.wait_ge(smm.h, ev[K("dmm", g)])
                eng.wait_ge(srp[p].h, ev[K("xrld", g)])
                if g >= 2:
                    eng.wait_ge(sop[p].h, ev[K("x1st", g - 2)])
                ev[K("x1ve", tt, oc)] = sve.inc(eng.scalar_tensor_tensor(
                    out=x1b[p][:, :], in0=ps_d[p][:, :],
                    scalar=sdh_sb[:, oc:oc + 1],
                    in1=xrb[p][:, :],
                    op0=OP.mult, op1=OP.add))

        def convg(ic):
            p = ic % 2
            eng.wait_ge(S_[f"vg_l{p}"].h, ev[K("g8ld", ic)])
            if ic >= 2:
                eng.wait_ge(smm.h, ev[K("guw", ic - 2)])
                eng.wait_ge(S_[f"vg_s{p}"].h, ev[K("gst", ic - 2)])
            ev[K("cvg", ic)] = sve.inc(
                eng.tensor_copy(out=wgb[p][:, :], in_=stg_g8[p][:, :]))

        ln(0)
        for tt in range(NT):
            if tt == 0 and do_conv:
                for ic in range(NIC):
                    convg(ic)
                    if ic >= 1:
                        hp1(tt, ic - 1)
                hp1(tt, NIC - 1)
                ev[K("cvg_last")] = ev[K("cvg", NIC - 1)]
            else:
                hpass(tt)
            if tt + 1 < NT:
                ln(tt + 1)
            x1pass(tt)

    def scalar(eng):
        def rstd(tt):
            eng.wait_ge(sve.h, ev[K("stats", tt)])
            sact.inc(eng.activation(out=a_sb[:, :], in_=v_sb[:, :], func=AF.Ln,
                                    bias=np.float32(1e-5)))
            eng.wait_ge(sact.h, sact.n)
            ev[K("rstd", tt)] = sact.inc(
                eng.activation(out=a_sb[:, :], in_=a_sb[:, :], func=AF.Exp,
                               scale=np.float32(-0.5)))

        def silu(tt):
            for ic in range(NIC):
                g = tt * NIC + ic
                eng.wait_ge(smm.h, ev[K("gmm", g)])
                if g >= 2:
                    eng.wait_ge(sve.h, ev[K("h", g - 2)])
                ev[K("silu", g)] = sact.inc(eng.activation(
                    out=gsb[g % 2][:, :], in_=ps_g[:, :],
                    func=AF.Silu, scale=sg_sb[:, ic:ic + 1]))

        def stores(tt):
            for oc in range(NOC):
                g = tt * NOC + oc
                p = g % 2
                if g >= 2:
                    eng.wait_ge(sve.h, ev[K("x1ve", (g - 2) // NOC, (g - 2) % NOC)])
                srp[p].inc(eng.dma_start(
                    out=xrb[p][:, :],
                    in_=src_d[oc * 128:(oc + 1) * 128, tt * TT:(tt + 1) * TT]), 16)
                ev[K("xrld", g)] = srp[p].n
                eng.wait_ge(sve.h, ev[K("x1ve", tt, oc)])
                ev[K("x1st", g)] = sop[p].inc(eng.dma_start(
                    out=dst_d[oc * 128:(oc + 1) * 128, tt * TT:(tt + 1) * TT],
                    in_=x1b[p][:, :]), 16)

        def silu1(tt, ic):
            g = tt * NIC + ic
            eng.wait_ge(smm.h, ev[K("gmm", g)])
            if g >= 2:
                eng.wait_ge(sve.h, ev[K("h", g - 2)])
            ev[K("silu", g)] = sact.inc(eng.activation(
                out=gsb[g % 2][:, :], in_=ps_g[:, :],
                func=AF.Silu, scale=sg_sb[:, ic:ic + 1]))

        rstd(0)
        for tt in range(NT):
            if tt == 0 and do_conv:
                for ic in range(NIC):
                    p = ic % 2
                    eng.wait_ge(sve.h, ev[K("cvg", ic)])
                    ev[K("gst", ic)] = S_[f"vg_s{p}"].inc(eng.dma_start(
                        out=wg_bf[ic * 128:(ic + 1) * 128, :],
                        in_=wgb[p][:, :]), 16)
                    eng.wait_ge(S_["e_po"].h, ev[K("cvu", ic)])
                    ev[K("ust", ic)] = S_[f"vu_s{p}"].inc(eng.dma_start(
                        out=wu_bf[ic * 128:(ic + 1) * 128, :],
                        in_=wub[p][:, :]), 16)
                    silu1(tt, ic)
                    if ic % 2 == 1 and ic >= 3:
                        c = ic // 2 - 1
                        p2 = c % 2
                        eng.wait_ge(S_[f"vd_l{p2}"].h, ev[K("d8ld", c)])
                        if c >= 2:
                            eng.wait_ge(S_[f"vd_s{p2}"].h, ev[K("dst", c - 2)])
                        ev[K("cvd", c)] = sact.inc(eng.activation(
                            out=cvd_out[p2][:, :], in_=stg_d8[p2][:, :],
                            func=AF.Copy))
                        eng.wait_ge(sact.h, sact.n)
                        oc, half = c // 2, c % 2
                        ev[K("dst", c)] = S_[f"vd_s{p2}"].inc(eng.dma_start(
                            out=wd_bf[oc * 128:(oc + 1) * 128,
                                      half * 4096:(half + 1) * 4096],
                            in_=cvd_out[p2][:, :]), 16)
                for c in (31,):
                    p2 = c % 2
                    eng.wait_ge(S_[f"vd_l{p2}"].h, ev[K("d8ld", c)])
                    eng.wait_ge(S_[f"vd_s{p2}"].h, ev[K("dst", c - 2)])
                    ev[K("cvd", c)] = sact.inc(eng.activation(
                        out=cvd_out[p2][:, :], in_=stg_d8[p2][:, :],
                        func=AF.Copy))
                    eng.wait_ge(sact.h, sact.n)
                    oc, half = c // 2, c % 2
                    ev[K("dst", c)] = S_[f"vd_s{p2}"].inc(eng.dma_start(
                        out=wd_bf[oc * 128:(oc + 1) * 128,
                                  half * 4096:(half + 1) * 4096],
                        in_=cvd_out[p2][:, :]), 16)
                ev[K("dst_last0")] = S_["vd_s0"].n
                ev[K("dst_last1")] = S_["vd_s1"].n
            else:
                silu(tt)
            if tt + 1 < NT:
                rstd(tt + 1)
            stores(tt)

    def gpsimd(eng):
        if not do_conv:
            return
        for ic in range(NIC):
            p = ic % 2
            eng.wait_ge(S_[f"vu_l{p}"].h, ev[K("u8ld", ic)])
            if ic >= 2:
                eng.wait_ge(smm.h, ev[K("guw", ic - 2)])
                eng.wait_ge(S_[f"vu_s{p}"].h, ev[K("ust", ic - 2)])
            ev[K("cvu", ic)] = S_["e_po"].inc(
                eng.tensor_copy(out=wub[p][:, :], in_=stg_u8[p][:, :]))
        ev[K("cvu_last")] = ev[K("cvu", NIC - 1)]

    return {"sync": sync, "tensor": tensor, "vector": vector, "scalar": scalar,
            "gpsimd": gpsimd}


def phase_qkv(S_, C, ev):
    ones_sb = C["ones_sb"]
    x1d = C["x1d"]; wpT = C["wpT"]; qkv_con = C["qkv_con"]
    xc = C["q_xc"]
    sqb = C["q_sq"]; xbb = C["q_xb"]
    m_sb = C["q_m"]; v_sb = C["q_v"]; a_sb = C["q_A"]; b_sb = C["q_B"]
    lntmp = C["q_lnt"]
    xn1 = C["q_xn"]
    wpf = C["q_wpf"]
    wpb = C["q_wpb"]
    ob = C["q_ob"]
    ps_sum2 = [C["PS"][0], C["PS"][2]]
    ps_sq2 = [C["PS"][1], C["PS"][3]]
    ps_q = [C["PS"][4], C["PS"][5]]
    sxp = [S_["q_x0"], S_["q_x1"]]
    swp = [S_["q_w0"], S_["q_w1"]]
    ssp = [S_["q_s0"], S_["q_s1"]]
    smm, sve, sact = S_["e_mm"], S_["e_ve"], S_["e_act"]

    LSEQ = []
    for th in range(4):
        LSEQ += [(0, th, dt) for dt in range(NDT)]
        LSEQ += [(1, th, dt) for dt in range(NDT)]

    OCSEQ = []
    OCROW = {}
    for g in range(4):
        for sec in range(3):
            for slot in range(4):
                oc = sec * 16 + slot * 4 + g
                OCSEQ.append(oc)
                OCROW[oc] = g * 1536 + sec * 512 + slot * 128

    def sync(eng):
        def wqld(k):
            oc = OCSEQ[k]
            p = k % 2
            if k >= 2:
                eng.wait_ge(sve.h, ev[("qcv", k - 2)])
            swp[p].inc(eng.dma_start(
                out=wpf[p][:, :], in_=wpT[oc * 128:(oc + 1) * 128, :]), 16)
            ev[("qwld", k)] = swp[p].n

        for j, key in enumerate(LSEQ):
            rnd, th, dt = key
            p = j % 2
            if j >= 2:
                eng.wait_ge(sve.h, ev[("qx",) + LSEQ[j - 2]])
            sxp[p].inc(eng.dma_start(
                out=xc[p][:, :],
                in_=x1d[dt * 128:(dt + 1) * 128, th * TT:(th + 1) * TT]), 16)
            ev[("qxld",) + key] = sxp[p].n
            if j == 31:
                for k in range(6):
                    wqld(k)
        for k in range(6, 48):
            wqld(k)

    def vector(eng):
        jof = {key: j for j, key in enumerate(LSEQ)}
        SCHED4 = [(k, th) for th in range(4) for k in range(6)] + \
                 [(k, th) for k in range(6, 48) for th in range(4)]
        lastpos = {}
        for i_, kt_ in enumerate(SCHED4):
            lastpos[kt_[0]] = kt_

        def qconv(k):
            p = k % 2
            eng.wait_ge(swp[p].h, ev[("qwld", k)])
            if k >= 6:
                eng.wait_ge(smm.h, ev[("qmm",) + lastpos[k - 6]])
            ev[("qcv", k)] = sve.inc(
                eng.tensor_copy(out=wpb[k % 6][:, :], in_=wpf[p][:, :]))
        for th in range(4):
            for dt in range(NDT):
                j = jof[(0, th, dt)]
                p = j % 2
                eng.wait_ge(sxp[p].h, ev[("qxld", 0, th, dt)])
                i = th * NDT + dt
                if i >= 2:
                    eng.wait_ge(smm.h, ev[("qstat", i - 2)])
                sve.inc(eng.tensor_copy(out=xbb[p][:, :], in_=xc[p][:, :]))
                ev[("qx", 0, th, dt)] = sve.inc(
                    eng.tensor_mul(out=sqb[p][:, :], in0=xc[p][:, :], in1=xc[p][:, :]))
            eng.wait_ge(smm.h, ev[("qstat", th * NDT + NDT - 1)])
            sve.inc(eng.tensor_scalar_mul(out=m_sb[:, :], in0=ps_sum2[th % 2][:, :],
                                          scalar1=float(1.0 / D)))
            eng.wait_ge(sve.h, sve.n)
            sve.inc(eng.scalar_tensor_tensor(out=v_sb[:, :], in0=m_sb[:, :],
                                             scalar=-1.0, in1=m_sb[:, :],
                                             op0=OP.mult, op1=OP.mult))
            eng.wait_ge(sve.h, sve.n)
            ev[("qvar", th)] = sve.inc(
                eng.scalar_tensor_tensor(out=v_sb[:, :], in0=ps_sq2[th % 2][:, :],
                                         scalar=float(1.0 / D), in1=v_sb[:, :],
                                         op0=OP.mult, op1=OP.add))
            eng.wait_ge(sact.h, ev[("qrstd", th)])
            sve.inc(eng.scalar_tensor_tensor(out=b_sb[:, :], in0=m_sb[:, :],
                                             scalar=-1.0, in1=a_sb[:, :],
                                             op0=OP.mult, op1=OP.mult))
            eng.wait_ge(sve.h, sve.n)
            for dt in range(NDT):
                j = jof[(1, th, dt)]
                p = j % 2
                eng.wait_ge(sxp[p].h, ev[("qxld", 1, th, dt)])
                xnt = xn1[:, dt * T + th * TT: dt * T + (th + 1) * TT]
                sve.inc(eng.tensor_mul(out=lntmp[:, :], in0=xc[p][:, :], in1=a_sb[:, :]))
                eng.wait_ge(sve.h, sve.n)
                ev[("qx", 1, th, dt)] = sve.inc(
                    eng.tensor_add(out=xnt, in0=lntmp[:, :], in1=b_sb[:, :]))
                eng.wait_ge(sve.h, sve.n)
            ev[("qxnth", th)] = sve.n
            if th == 0:
                for k in range(6):
                    qconv(k)
        ev[("qxn",)] = sve.n
        for k in range(6, 48):
            qconv(k)

    def tensor(eng):
        for th in range(4):
            for dt in range(NDT):
                i = th * NDT + dt
                p = (th * 32 + dt) % 2
                eng.wait_ge(sve.h, ev[("qx", 0, th, dt)])
                if dt == 0 and th > 1:
                    eng.wait_ge(sve.h, ev[("qvar", th - 2)])
                eng.matmul(ps_sum2[th % 2][:, :], ones_sb[:, :], xbb[p][:, :],
                           start=(dt == 0), stop=(dt == NDT - 1))
                mmi = eng.matmul(ps_sq2[th % 2][:, :], ones_sb[:, :], sqb[p][:, :],
                                 start=(dt == 0), stop=(dt == NDT - 1))
                ev[("qstat", i)] = smm.inc(mmi)
        SCHED = [(k, th) for th in range(4) for k in range(6)] + \
                [(k, th) for k in range(6, 48) for th in range(4)]
        C["q_sched"] = SCHED
        for i, (k, th) in enumerate(SCHED):
            wp = wpb[k % 6]
            eng.wait_ge(sve.h, ev[("qcv", k)])
            pp = i % 2
            eng.wait_ge(sve.h, ev[("qxnth", th)])
            if i >= 2:
                eng.wait_ge(sact.h, ev[("qev",) + SCHED[i - 2]])
            for dt in range(NDT):
                mmi = eng.matmul(ps_q[pp][:, :], wp[:, dt * 128:(dt + 1) * 128],
                                 xn1[:, dt * T + th * TT: dt * T + (th + 1) * TT],
                                 start=(dt == 0), stop=(dt == NDT - 1))
            ev[("qmm", k, th)] = smm.inc(mmi)

    def scalar(eng):
        for th in range(4):
            eng.wait_ge(sve.h, ev[("qvar", th)])
            sact.inc(eng.activation(out=a_sb[:, :], in_=v_sb[:, :], func=AF.Ln,
                                    bias=np.float32(1e-5)))
            eng.wait_ge(sact.h, sact.n)
            ev[("qrstd", th)] = sact.inc(
                eng.activation(out=a_sb[:, :], in_=a_sb[:, :], func=AF.Exp,
                               scale=np.float32(-0.5)))
        SCHED = C["q_sched"]
        done_k = {}
        for i, (k, th) in enumerate(SCHED):
            oc = OCSEQ[k]
            row = OCROW[oc]
            pp = i % 2
            eng.wait_ge(smm.h, ev[("qmm", k, th)])
            if i >= 2:
                eng.wait_ge(ssp[pp].h, ev[("qst", i - 2)])
            scale = np.float32(QSCALE) if oc < 16 else np.float32(1.0)
            ev[("qev", k, th)] = sact.inc(eng.activation(
                out=ob[pp][:, :], in_=ps_q[pp][:, :], func=AF.Copy, scale=scale))
            eng.wait_ge(sact.h, sact.n)
            ev[("qst", i)] = ssp[pp].inc(eng.dma_start(
                out=qkv_con[row:row + 128, th * TT:(th + 1) * TT],
                in_=ob[pp][:, :]), 16)
            done_k[k] = done_k.get(k, 0) + 1
            if done_k[k] == 4 and all(done_k.get(kk, 0) == 4
                                      for kk in range(12 * (k // 12), k + 1)) \
               and (k % 12 == 11 or False):
                g = k // 12
                eng.wait_ge(ssp[0].h, ssp[0].n)
                eng.wait_ge(ssp[1].h, ssp[1].n)
                ev[("qgdone", g)] = sact.inc(eng.activation(
                    out=C["q_dum"][:, :], in_=C["q_dum"][:, :], func=AF.Copy))

    def gpsimd(eng):
        for g in range(4):
            eng.wait_ge(sact.h, ev[("qgdone", g)])
            S_[f"cc1g{g}"].inc(eng.collective_compute(
                "AllGather", OP.bypass, replica_groups=[[0, 1, 2, 3], [4, 5, 6, 7]],
                ins=[qkv_con.ap()[g * 1536:(g + 1) * 1536, :].opt()],
                outs=[C["qkv_gat"].ap()[g * 6144:(g + 1) * 6144, :].opt()]))

    return {"sync": sync, "vector": vector, "tensor": tensor, "scalar": scalar,
            "gpsimd": gpsimd}


def phase_cc(S_, C, ev, which):
    groups = [[0, 1, 2, 3], [4, 5, 6, 7]]

    def gpsimd(eng):
        if which == 1:
            cin, cout, sem = C["qkv_con"], C["qkv_gat"], S_["cc1"]
        else:
            cin, cout, sem = C["o_con"], C["o_gat"], S_["cc2"]
        sem.inc(eng.collective_compute(
            "AllGather", OP.bypass, replica_groups=groups,
            ins=[cin.ap().opt()], outs=[cout.ap().opt()]))
        eng.wait_ge(sem.h, sem.n)

    return {"gpsimd": gpsimd}


def phase_attn(S_, C, ev):
    ident_sb = C["ident_sb"]; mask_sb = C["mask_sb"]
    qkv_gat = C["qkv_gat"]; o_con = C["o_con"]
    ones_sb = C["ones_sb"]
    q2 = C["at_q"]; k2 = C["at_k"]; vf2 = C["at_vf"]; vt2 = C["at_vt"]
    pstage = C["at_ps"]
    scr = C["at_scr"]
    rsb = C["at_rs"]
    o_acc = C["at_o"]
    ps_s4 = C["PS"][0:4]
    ps_sm = [C["PS"][4], C["PS"][5]]
    ps_o = [C["PS"][6], C["PS"][7]]
    sldp = [S_["a_l0"], S_["a_l1"]]
    sst = S_["a_st"]
    smm, sve, sact = S_["e_mm"], S_["e_ve"], S_["e_act"]

    def geom(p):
        nk2 = 2 * p + 2
        return nk2, nk2 // 2

    def pair_units(p):
        ngrp = geom(p)[1]
        if ngrp == 1:
            return [(p, 0)]
        return [(p, ngrp - 1)] + [(p, g) for g in range(ngrp - 1)]

    def last_unit(p):
        ngrp = geom(p)[1]
        return (p, ngrp - 2) if ngrp >= 2 else (p, 0)

    LASTG = ("apvg", NH_LOC - 1, 31, geom(31)[1] - 1)

    def sync(eng):
        with eng.register("at_m") as rm, eng.register("at_off") as ro:
            pid = eng.partition_id()
            eng.reg_mov(rm, pid)
            eng.reg_mod(rm, rm, 4)
            eng.reg_mul(rm, rm, 128 * T)
            for hl in range(NH_LOC):
                par = hl % 2
                if hl > 1:
                    eng.wait_ge(smm.h, ev[("apv_last", hl - 2)])
                eng.wait_ge(S_[f"cc1g{hl}"].h, 1)
                for sec, dsts in ((0, q2), (1, k2), (2, vf2)):
                    base = (hl * 6144 + sec * 512) * T
                    eng.reg_add(ro, rm, base)
                    srcap = (bass.AP(qkv_gat, ro,
                                     [[T, 128], [1536 * T, 4], [1, T]])
                             if C["emit"] else None)
                    dst3 = dsts[par][:, :].rearrange("p (r t) -> p r t", r=4)
                    sldp[par].inc(eng.dma_start(out=dst3, in_=srcap), 16)
                ev[("ald", hl)] = sldp[par].n

    def tensor(eng):
        relsc = {}
        for hl in range(NH_LOC):
            par = hl % 2
            q_sb, k_sb, vf_sb, vt_sb = q2[par], k2[par], vf2[par], vt2[par]
            eng.wait_ge(sldp[par].h, ev[("ald", hl)])
            for vg in range(8):
                pb = vg % 4
                r = relsc.get(pb)
                if r is not None:
                    eng.wait_ge(r[0].h, ev[r[1]])
                for j in range(8):
                    kt = vg * 8 + j
                    mmi = eng.transpose(
                        ps_s4[pb].bitcast(BF16)[:, j * 128:(j + 1) * 128],
                        vf_sb[:, kt * 128:(kt + 1) * 128], ident_sb[:, :])
                ev[("avt", hl, vg)] = smm.inc(mmi)
                relsc[pb] = (sve, ("avte", hl, vg))

            units = []
            for p in range(32):
                units += pair_units(p)

            ndone = {}

            def consume(u):
                p, g = u
                nk2, ngrp = geom(p)
                pb = ev[("aunit_pb", hl, p, g)]
                eng.wait_ge(sact.h, ev[("aexp", hl, p, g)])
                if ndone.get(p, 0) == 0:
                    if p >= 2:
                        eng.wait_ge(sve.h, ev[("aoev", hl, p - 2)])
                    elif hl > 0:
                        eng.wait_ge(sve.h, ev[("aoev", hl - 1, 30 + p)])
                for j in range(2):
                    kt = g * 2 + j
                    pst = pstage[pb][:, j * 256:(j + 1) * 256]
                    nd_ = ndone[p] = ndone.get(p, 0) + 1
                    eng.matmul(ps_sm[p % 2][:, 0:256], ones_sb[:, :], pst,
                               start=(nd_ == 1), stop=(nd_ == nk2))
                    mmi = eng.matmul(ps_o[p % 2][:, 0:256],
                                     vt_sb[:, kt * 128:(kt + 1) * 128], pst,
                                     start=(nd_ == 1), stop=(nd_ == nk2))
                ev[("apvg", hl, p, g)] = smm.inc(mmi)

            pend = []
            for ui, u in enumerate(units):
                p, g = u
                pb = ui % 4
                ev[("aunit_pb", hl, p, g)] = pb
                qs = q_sb[:, 256 * p:256 * (p + 1)]
                r = relsc.get(pb)
                if r is not None:
                    eng.wait_ge(r[0].h, ev[r[1]])
                for j in range(2):
                    kt = g * 2 + j
                    mmi = eng.matmul(
                        ps_s4[pb][:, j * 256:(j + 1) * 256],
                        k_sb[:, kt * 128:(kt + 1) * 128], qs,
                        start=True, stop=True)
                ev[("asc", hl, p, g)] = smm.inc(mmi)
                relsc[pb] = (sact, ("aexp", hl, p, g))
                pend.append(u)
                if len(pend) > 2:
                    consume(pend.pop(0))
            while pend:
                consume(pend.pop(0))
            ev[("asc_last", hl)] = ev[("asc", hl) + last_unit(31)]
            ev[("apv_last", hl)] = ev[("apvg", hl) + last_unit(31)]

    def scalar(eng):
        pstrel = {}
        for hl in range(NH_LOC):
            for p in range(32):
                nk2, ngrp = geom(p)
                for (pp_, g) in pair_units(p):
                    pb = ev[("aunit_pb", hl, p, g)]
                    eng.wait_ge(smm.h, ev[("asc", hl, p, g)])
                    r = pstrel.get(pb)
                    if r is not None:
                        eng.wait_ge(smm.h, ev[r])
                    if g == ngrp - 1:
                        eng.wait_ge(sve.h, ev[("amask", hl, p)])
                        last = eng.activation(out=pstage[pb][:, :],
                                              in_=scr[:, :], func=AF.Exp)
                    else:
                        last = eng.activation(out=pstage[pb][:, :],
                                              in_=ps_s4[pb][:, :], func=AF.Exp)
                    ev[("aexp", hl, p, g)] = sact.inc(last)
                    pstrel[pb] = ("apvg", hl, p, g)
            eng.wait_ge(sve.h, ev[("aoev", hl, 31)])
            ev[("aost", hl)] = sst.inc(eng.dma_start(
                out=o_con[hl * 128:(hl + 1) * 128, :],
                in_=o_acc[:, hl * S:(hl + 1) * S]), 16)
            eng.wait_ge(sst.h, ev[("aost", hl)])
            ev[("agdone", hl)] = sact.inc(eng.activation(
                out=C["at_dum"][:, :], in_=C["at_dum"][:, :], func=AF.Copy))

    def gpsimd(eng):
        for hl in range(NH_LOC):
            eng.wait_ge(sact.h, ev[("agdone", hl)])
            S_[f"cc2g{hl}"].inc(eng.collective_compute(
                "AllGather", OP.bypass, replica_groups=[[0, 1, 2, 3], [4, 5, 6, 7]],
                ins=[o_con.ap()[hl * 128:(hl + 1) * 128, :].opt()],
                outs=[C["o_gat"].ap()[hl * 512:(hl + 1) * 512, :].opt()]))

    def vector(eng):
        for hl in range(NH_LOC):
            par = hl % 2
            vt_sb = vt2[par]
            for vg in range(8):
                pb = vg % 4
                eng.wait_ge(smm.h, ev[("avt", hl, vg)])
                if hl > 1 and vg == 0:
                    eng.wait_ge(smm.h, ev[("apv_last", hl - 2)])
                ev[("avte", hl, vg)] = sve.inc(eng.tensor_copy(
                    out=vt_sb[:, vg * 1024:(vg + 1) * 1024],
                    in_=ps_s4[pb].bitcast(BF16)[:, 0:1024]))
            def amask(p):
                nk2, ngrp = geom(p)
                glast = ngrp - 1
                eng.wait_ge(smm.h, ev[("asc", hl, p, glast)])
                if p > 0 or hl > 0:
                    prevkey = (("aexp", hl, p - 1, geom(p - 1)[1] - 1) if p > 0
                               else ("aexp", hl - 1, 31, geom(31)[1] - 1))
                    eng.wait_ge(sact.h, ev[prevkey])
                pbl = ev[("aunit_pb", hl, p, glast)]
                ev[("amask", hl, p)] = sve.inc(eng.tensor_add(
                    out=scr[:, :], in0=ps_s4[pbl][:, :], in1=mask_sb[:, :]))

            amask(0)
            for p in range(32):
                lu = last_unit(p)
                eng.wait_ge(smm.h, ev[("apvg", hl) + lu])
                if p >= 2:
                    eng.wait_ge(sve.h, ev[("aoev", hl, p - 2)])
                elif hl > 0:
                    eng.wait_ge(sve.h, ev[("aoev", hl - 1, 30 + p)])
                sve.inc(eng.reciprocal(out=rsb[p % 2][:, :],
                                       in_=ps_sm[p % 2][:, 0:256]))
                eng.wait_ge(sve.h, sve.n)
                ev[("aoev", hl, p)] = sve.inc(eng.tensor_mul(
                    out=o_acc[:, (hl * 32 + p) * 256:(hl * 32 + p + 1) * 256],
                    in0=ps_o[p % 2][:, 0:256], in1=rsb[p % 2][:, :]))
                if p < 31:
                    amask(p + 1)

    return {"sync": sync, "tensor": tensor, "scalar": scalar, "vector": vector,
            "gpsimd": gpsimd}


def phase_oproj(S_, C, ev):
    o_gat = C["o_gat"]; woT = C["woT"]; x1d = C["x1d"]; x2d = C["x2d"]
    o_sb = C["op_o"]
    wof = C["op_wf"]; wob = C["op_wb"]
    x1c = C["op_x1"]
    x2b = C["op_x2"]
    ps = [C["PS"][0], C["PS"][1]]
    solg = [S_["o_ld"], S_["q_x0"], S_["o_g2"], S_["a_l0"]]
    sxp = [S_["o_x0"], S_["o_x1"]]
    swp = [S_["q_w0"], S_["q_w1"]]
    ssp = [S_["o_s0"], S_["o_s1"]]
    smm, sve, sact = S_["e_mm"], S_["e_ve"], S_["e_act"]

    def oload(eng, hl, half=None):
        tag = f"{hl}{half}"
        with eng.register(f"op_m{tag}") as rm, eng.register(f"op_o{tag}") as ro:
            pid = eng.partition_id()
            eng.reg_mov(rm, pid)
            eng.reg_mod(rm, rm, 4)
            eng.reg_mul(rm, rm, T)
            eng.wait_ge(S_[f"cc2g{hl}"].h, 1)
            hof = 0 if half is None or half == 0 else T // 2
            eng.reg_add(ro, rm, hl * 512 * S + hof)
            tlen = T if half is None else T // 2
            srcap = (bass.AP(o_gat, ro,
                             [[S, 128], [128 * S, 4], [1, tlen]])
                     if C["emit"] else None)
            dst3 = o_sb[:, :].rearrange(
                "p (r f t) -> p r f t", r=4, f=4)[:, :, hl, :]
            if half is not None:
                dst3 = dst3[:, :, hof:hof + tlen]
            solg[hl].inc(eng.dma_start(out=dst3, in_=srcap), 16)
            ev[("oldg", hl, half)] = (hl, solg[hl].n)

    def sync(eng):
        oload(eng, 0)
        oload(eng, 3, half=0)
        for oc in range(NOC):
            p = oc % 2
            if oc >= 2:
                eng.wait_ge(sve.h, ev[("ocv", oc - 2)])
            swp[p].inc(eng.dma_start(
                out=wof[p][:, :], in_=woT[oc * 128:(oc + 1) * 128, :]), 16)
            ev[("owld", oc)] = swp[p].n

    def scalar(eng):
        oload(eng, 1)
        oload(eng, 3, half=1)
        for i in range(NOC * 4):
            oc, th = i // 4, i % 4
            p = i % 2
            if i >= 2:
                eng.wait_ge(sve.h, ev[("ox2", i - 2)])
            ssp[p].inc(eng.dma_start(
                out=x1c[p][:, :],
                in_=x1d[oc * 128:(oc + 1) * 128, th * TT:(th + 1) * TT]), 16)
            ev[("ox1ld", i)] = ssp[p].n

    def vector(eng):
        def conv(oc):
            p = oc % 2
            eng.wait_ge(swp[p].h, ev[("owld", oc)])
            if oc >= 2:
                eng.wait_ge(smm.h, ev[("omm", oc - 2, 3)])
            ev[("ocv", oc)] = sve.inc(
                eng.tensor_copy(out=wob[p][:, :], in_=wof[p][:, :]))

        conv(0)
        conv(1)
        for oc in range(NOC):
            for th in range(4):
                i = oc * 4 + th
                p = i % 2
                eng.wait_ge(smm.h, ev[("omm", oc, th)])
                eng.wait_ge(ssp[p].h, ev[("ox1ld", i)])
                if i >= 2:
                    eng.wait_ge(S_[f"o_t{(i - 2) % 2}"].h, ev[("ox2st", i - 2)])
                ev[("ox2", i)] = sve.inc(eng.tensor_add(
                    out=x2b[p][:, :], in0=ps[p][:, :], in1=x1c[p][:, :]))
            if oc + 2 < NOC:
                conv(oc + 2)

    def gpsimd(eng):
        oload(eng, 2)
        for i in range(NOC * 4):
            oc, th = i // 4, i % 4
            p = i % 2
            eng.wait_ge(sve.h, ev[("ox2", i)])
            ev[("ox2st", i)] = S_[f"o_t{p}"].inc(eng.dma_start(
                out=x2d[oc * 128:(oc + 1) * 128, th * TT:(th + 1) * TT],
                in_=x2b[p][:, :]), 16)

    DTSEQ = [g + 4 * slot for g in range(4) for slot in range(4)]

    def tensor(eng):
        for oc in range(NOC):
            wo = wob[oc % 2]
            eng.wait_ge(sve.h, ev[("ocv", oc)])
            for th in range(4):
                i = oc * 4 + th
                p = i % 2
                if i >= 2:
                    eng.wait_ge(sve.h, ev[("ox2", i - 2)])
                for j, dt in enumerate(DTSEQ):
                    if i == 0 and j % 4 == 0 and C["emit"]:
                        hl_ = j // 4
                        if hl_ == 3:
                            g_, v_ = ev[("oldg", 3, 1)]
                            eng.wait_ge(solg[g_].h, v_)
                        else:
                            g_, v_ = ev[("oldg", hl_, None)]
                            eng.wait_ge(solg[g_].h, v_)
                    mmi = eng.matmul(ps[p][:, :], wo[:, dt * 128:(dt + 1) * 128],
                                     o_sb[:, dt * T + th * TT: dt * T + (th + 1) * TT],
                                     start=(j == 0), stop=(j == NDT - 1))
                ev[("omm", oc, th)] = smm.inc(mmi)

    return {"sync": sync, "scalar": scalar, "vector": vector,
            "gpsimd": gpsimd, "tensor": tensor}


def phase_final(S_, C, ev):
    def gpsimd(eng):
        for p in range(2):
            eng.wait_ge(S_[f"f_o{p}"].h, S_[f"f_o{p}"].n)

    return {"gpsimd": gpsimd}


def make_phases(S_, C, ev):
    ph = [phase_const(S_, C, ev), "BARRIER"]
    if "ffn1" in PHASES:
        ph += [phase_ffn(S_, C, ev, C["x_fm"],
                         C["out_fm"] if C["dev_out"] == "x1" else C["x1d"], "f1",
                         do_conv=True),
               "BARRIER"]
    if "qkv" in PHASES:
        ph += [phase_qkv(S_, C, ev), "BARRIER"]
    if "attn" in PHASES:
        ph += [phase_attn(S_, C, ev), "BARRIER"]
    if "oproj" in PHASES:
        ph += [phase_oproj(S_, C, ev), "BARRIER"]
    if "ffn2" in PHASES:
        ph += [phase_ffn(S_, C, ev, C["x2d"], C["out_fm"], "f2"), "BARRIER"]
    ph.append(phase_final(S_, C, ev))
    return ph


def build(dev_out=None):
    nc = bass.Bass("TRN2", target_bir_lowering=False, debug=False)

    C = {"dev_out": dev_out}
    C["x_fm"] = nc.dram_tensor("x_fm", [D, T], F32, kind="ExternalInput").ap()
    C["wg8"] = nc.dram_tensor("wg8", [NIC * 128, D], I8, kind="ExternalInput").ap()
    C["wu8"] = nc.dram_tensor("wu8", [NIC * 128, D], I8, kind="ExternalInput").ap()
    C["wd8"] = nc.dram_tensor("wd8", [NOC * 128, INTER], I8, kind="ExternalInput").ap()
    C["wpT"] = nc.dram_tensor("wpT", [48 * 128, D], F32, kind="ExternalInput").ap()
    C["woT"] = nc.dram_tensor("woT", [16 * 128, D], F32, kind="ExternalInput").ap()
    sg_in = nc.dram_tensor("sg_in", [128, NIC], F32, kind="ExternalInput").ap()
    su_in = nc.dram_tensor("su_in", [128, NIC], F32, kind="ExternalInput").ap()
    sd_in = nc.dram_tensor("sd_in", [128, NOC], F32, kind="ExternalInput").ap()
    ones_in = nc.dram_tensor("ones_in", [128, 128], BF16, kind="ExternalInput").ap()
    ident_in = nc.dram_tensor("ident_in", [128, 128], BF16, kind="ExternalInput").ap()
    mask_in = nc.dram_tensor("mask_in", [128, 512], F32, kind="ExternalInput").ap()
    C["out_fm"] = nc.dram_tensor("out_fm", [D, T], F32, kind="ExternalOutput").ap()

    C["wg_bf"] = nc.dram_tensor("wg_bf", [NIC * 128, D], BF16, kind="Internal").ap()
    C["wu_bf"] = nc.dram_tensor("wu_bf", [NIC * 128, D], BF16, kind="Internal").ap()
    C["wd_bf"] = nc.dram_tensor("wd_bf", [NOC * 128, INTER], BF16, kind="Internal").ap()
    C["x1d"] = nc.dram_tensor("x1d", [D, T], F32, kind="Internal").ap()
    C["x2d"] = nc.dram_tensor("x2d", [D, T], F32, kind="Internal").ap()
    C["qkv_con"] = nc.dram_tensor("qkv_con", [3 * D, T], BF16, kind="Internal")
    C["qkv_gat"] = nc.dram_tensor("qkv_gat", [4 * 3 * D, T], BF16, kind="Internal")
    C["o_con"] = nc.dram_tensor("o_con", [NH_LOC * 128, S], BF16, kind="Internal")
    C["o_gat"] = nc.dram_tensor("o_gat", [4 * NH_LOC * 128, S], BF16, kind="Internal")

    from contextlib import ExitStack
    with ExitStack() as ctx:
        E = ctx.enter_context

        C["ones_sb"] = E(nc.sbuf_tensor("ones_sb", [128, 128], BF16))
        C["ident_sb"] = E(nc.sbuf_tensor("ident_sb", [128, 128], BF16))
        C["mask_sb"] = E(nc.sbuf_tensor("mask_sb", [128, 512], F32))
        C["sg_sb"] = E(nc.sbuf_tensor("sg_sb", [128, NIC], F32))
        C["su_sb"] = E(nc.sbuf_tensor("su_sb", [128, NIC], F32))
        C["sdh_sb"] = E(nc.sbuf_tensor("sdh_sb", [128, NOC], F32))
        C["const_pairs"] = (
            (ones_in, C["ones_sb"]), (ident_in, C["ident_sb"]),
            (mask_in, C["mask_sb"]), (sg_in, C["sg_sb"]), (su_in, C["su_sb"]),
            (sd_in, C["sdh_sb"]),
        )

        ARENA_B = 204 * 1024
        arena_t = E(nc.sbuf_tensor("arena", [128, ARENA_B], I8))
        arena = arena_t.ap()

        def carve(off, nbytes, dtype, cols):
            ap = arena[:, off[0]:off[0] + nbytes].bitcast(dtype)
            assert ap.shape[1] == cols, (ap.shape, cols)
            off[0] += (nbytes + 31) & ~31
            assert off[0] <= ARENA_B, off[0]
            return ap

        o = [0]
        C["cv_g8"] = [carve(o, 2048, I8, 2048) for _ in range(2)]
        C["cv_gb"] = [carve(o, 4096, BF16, 2048) for _ in range(2)]
        C["cv_u8"] = [carve(o, 2048, I8, 2048) for _ in range(2)]
        C["cv_ub"] = [carve(o, 4096, BF16, 2048) for _ in range(2)]
        C["cv_d8"] = [carve(o, 8192, I8, 8192) for _ in range(2)]
        C["cv_db"] = [carve(o, 16384, BF16, 8192) for _ in range(2)]

        o = [0]
        C["ffn_xf"] = carve(o, NDT * TT * 4, F32, NDT * TT)
        C["ffn_xn2"] = [carve(o, NDT * TT * 2, BF16, NDT * TT) for _ in range(2)]
        C["ffn_xr"] = [carve(o, TT * 4, F32, TT) for _ in range(2)]
        C["ffn_h"] = carve(o, NIC * TT * 2, BF16, NIC * TT)
        C["ffn_wg"] = [carve(o, 4096, BF16, 2048) for _ in range(2)]
        C["ffn_wu"] = [carve(o, 4096, BF16, 2048) for _ in range(2)]
        C["ffn_wd"] = [carve(o, 16384, BF16, 8192) for _ in range(2)]
        C["ffn_g"] = [carve(o, TT * 4, F32, TT) for _ in range(2)]
        C["ffn_sq"] = [carve(o, TT * 2, BF16, TT) for _ in range(2)]
        C["ffn_xb"] = [carve(o, TT * 2, BF16, TT) for _ in range(2)]
        C["ffn_lnt"] = carve(o, TT * 4, F32, TT)
        C["ffn_m"] = carve(o, TT * 4, F32, TT)
        C["ffn_v"] = carve(o, TT * 4, F32, TT)
        C["ffn_A"] = carve(o, TT * 4, F32, TT)
        C["ffn_B"] = carve(o, TT * 4, F32, TT)
        C["ffn_x1"] = [carve(o, TT * 4, F32, TT) for _ in range(2)]

        o = [0]
        C["q_xn"] = carve(o, NDT * T * 2, BF16, NDT * T)
        C["q_xc"] = [carve(o, TT * 4, F32, TT) for _ in range(2)]
        C["q_sq"] = [carve(o, TT * 2, BF16, TT) for _ in range(2)]
        C["q_xb"] = [carve(o, TT * 2, BF16, TT) for _ in range(2)]
        C["q_m"] = carve(o, TT * 4, F32, TT)
        C["q_v"] = carve(o, TT * 4, F32, TT)
        C["q_A"] = carve(o, TT * 4, F32, TT)
        C["q_B"] = carve(o, TT * 4, F32, TT)
        C["q_lnt"] = carve(o, TT * 4, F32, TT)
        C["q_dum"] = carve(o, 4, F32, 1)
        C["q_wpf"] = [carve(o, 8192, F32, 2048) for _ in range(2)]
        C["q_wpb"] = [carve(o, 4096, BF16, 2048) for _ in range(6)]
        C["q_ob"] = [carve(o, TT * 2, BF16, TT) for _ in range(2)]

        o = [0]
        C["at_q"] = [carve(o, S * 2, BF16, S) for _ in range(2)]
        C["at_k"] = [carve(o, S * 2, BF16, S) for _ in range(2)]
        C["at_vf"] = [carve(o, S * 2, BF16, S) for _ in range(2)]
        C["at_vt"] = [carve(o, S * 2, BF16, S) for _ in range(2)]
        C["at_o"] = carve(o, NH_LOC * S * 2, BF16, NH_LOC * S)
        C["at_ps"] = [carve(o, 512 * 2, BF16, 512) for _ in range(4)]
        C["at_scr"] = carve(o, 512 * 4, F32, 512)
        C["at_rs"] = [carve(o, 256 * 4, F32, 256) for _ in range(2)]
        C["at_dum"] = carve(o, 4, F32, 1)

        o = [0]
        C["op_o"] = carve(o, NDT * T * 2, BF16, NDT * T)
        C["op_wf"] = [carve(o, 8192, F32, 2048) for _ in range(2)]
        C["op_wb"] = [carve(o, 4096, BF16, 2048) for _ in range(2)]
        C["op_x1"] = [carve(o, TT * 4, F32, TT) for _ in range(2)]
        C["op_x2"] = [carve(o, TT * 4, F32, TT) for _ in range(2)]

        C["PSW"] = [E(nc.psum_tensor(f"psw{i}", [128, 1024], F32)) for i in range(2)]
        C["PS"] = [C["PSW"][0][:, 0:512], C["PSW"][0][:, 512:1024],
                   C["PSW"][1][:, 0:512], C["PSW"][1][:, 512:1024]] + \
                  [E(nc.psum_tensor(f"ps{i}", [128, 512], F32)) for i in range(4, 8)]

        ev = EvDict()
        ev.strict = False
        C["emit"] = False
        S0 = {k: Sem(None) for k in SEM_NAMES}
        mock = _MockEng()
        snapshots = []
        phases0 = make_phases(S0, C, ev)
        for ph in phases0:
            if ph == "BARRIER":
                snapshots.append({k: s.n for k, s in S0.items()})
                continue
            for name in ENGS:
                if name in ph:
                    ph[name](mock)
        final_counts = {k: s.n for k, s in S0.items()}

        ev.strict = True
        C["emit"] = True
        S1 = {k: Sem(ctx.enter_context(nc.semaphore(k))) for k in SEM_NAMES}
        phases1 = make_phases(S1, C, ev)
        bi = [0]

        def emit_engine(eng, name):
            b = 0
            for ph in phases1:
                if ph == "BARRIER":
                    snap = snapshots[b]
                    b += 1
                    for k, v in snap.items():
                        if v > 0 and not k.startswith(("cc1g", "cc2g")):
                            eng.wait_ge(S1[k].h, v)
                    continue
                if name in ph:
                    ph[name](eng)

        with nc.Block() as block:
            block.sync(lambda eng: emit_engine(eng, "sync"))
            block.tensor(lambda eng: emit_engine(eng, "tensor"))
            block.vector(lambda eng: emit_engine(eng, "vector"))
            block.scalar(lambda eng: emit_engine(eng, "scalar"))
            block.gpsimd(lambda eng: emit_engine(eng, "gpsimd"))

        for k in SEM_NAMES:
            assert final_counts[k] == S1[k].n, (k, final_counts[k], S1[k].n)

    return nc


def _prep(inputs):
    x = np.ascontiguousarray(np.asarray(inputs["L0_attn_x"], f32))
    wg = np.asarray(inputs["L0_ffn_w_gate"], np.int8)
    wu = np.asarray(inputs["L0_ffn_w_up"], np.int8)
    wd = np.asarray(inputs["L0_ffn_w_down"], np.int8)
    wp = np.asarray(inputs["L0_attn_w_project"], f32)
    wo = np.asarray(inputs["L0_attn_w_out"], f32)
    sg = np.asarray(inputs["L0_ffn_s_gate"], f32)
    su = np.asarray(inputs["L0_ffn_s_up"], f32)
    sd = np.asarray(inputs["L0_ffn_s_down"], f32)

    wg8 = np.ascontiguousarray(
        wg.reshape(16, 128, 64, 128).transpose(2, 1, 0, 3).reshape(64 * 128, 2048))
    wu8 = np.ascontiguousarray(
        wu.reshape(16, 128, 64, 128).transpose(2, 1, 0, 3).reshape(64 * 128, 2048))
    wd8 = np.ascontiguousarray(
        wd.reshape(64, 128, 16, 128).transpose(2, 1, 0, 3).reshape(16 * 128, 8192))
    wpT = np.ascontiguousarray(
        wp.reshape(48, 128, 16, 128).transpose(0, 3, 2, 1).reshape(48 * 128, 2048))
    woT = np.ascontiguousarray(
        wo.reshape(16, 128, 16, 128).transpose(0, 3, 2, 1).reshape(16 * 128, 2048))
    sg2 = np.ascontiguousarray(sg.reshape(64, 128).T)
    su2 = np.ascontiguousarray(su.reshape(64, 128).T)
    sd2 = np.ascontiguousarray(sd.reshape(16, 128).T)
    ones = np.ones((128, 128), dtype=BF)
    ident = np.eye(128, dtype=BF)
    kk = np.arange(128)[:, None]
    qq = np.arange(256)[None, :]
    mA = np.where(kk > qq, f32(-1e9), f32(0.0))
    mB = np.where(kk + 128 > qq, f32(-1e9), f32(0.0))
    mask = np.concatenate([mA, mB], axis=1).astype(f32)

    common = dict(wg8=wg8, wu8=wu8, wd8=wd8, wpT=wpT, woT=woT,
                  sg_in=sg2, su_in=su2, sd_in=sd2,
                  ones_in=ones, ident_in=ident, mask_in=mask)
    in_maps = []
    for c in range(8):
        b, m = c // 4, c % 4
        xc = np.ascontiguousarray(x[b, m * T:(m + 1) * T, :].T)
        in_maps.append(dict(common, x_fm=xc))
    return in_maps


def kernel(**inputs):
    in_maps = _prep(inputs)
    nc = build()
    res = run_bass_kernel_spmd(nc, in_maps, core_ids=list(range(8)))
    out = np.empty((2, S, D), f32)
    for c in range(8):
        b, m = c // 4, c % 4
        out[b, m * T:(m + 1) * T, :] = res.results[c]["out_fm"].T
    return out


# version 12
# speedup vs baseline: 1.0860x; 1.0860x over previous
import numpy as np
import ml_dtypes
import concourse.bass as bass
import concourse.mybir as mybir
from concourse.bass_utils import run_bass_kernel_spmd

F32 = mybir.dt.float32
BF16 = mybir.dt.bfloat16
I8 = mybir.dt.int8
AF = mybir.ActivationFunctionType
OP = mybir.AluOpType
AX = mybir.AxisListType
BF = ml_dtypes.bfloat16
f32 = np.float32

D = 2048
T = 2048
S = 8192
INTER = 8192
NDT = 16
NIC = 64
NOC = 16
NT = 4
TT = 512
NH_LOC = 4
HD = 128
NKT = S // 128
QSCALE = float(1.0 / np.sqrt(HD))

ENGS = ("sync", "tensor", "vector", "scalar", "gpsimd")

PHASES = ("conv", "ffn1", "qkv", "attn", "oproj", "ffn2")


class Sem:
    __slots__ = ("h", "n")

    def __init__(self, h=None):
        self.h = h
        self.n = 0

    def inc(self, inst, k=1):
        if self.h is not None:
            inst.then_inc(self.h, k)
        self.n += k
        return self.n


class EvDict(dict):
    strict = True

    def __missing__(self, key):
        if self.strict:
            raise KeyError(key)
        return 0


class _MockInst:
    __slots__ = ()

    def then_inc(self, sem, k=1):
        return self


class _MockReg:
    pass


class _MockEng:
    def wait_ge(self, sem, val):
        pass

    def partition_id(self):
        return 0

    def register(self, name=None):
        from contextlib import nullcontext
        return nullcontext(_MockReg())

    def __getattr__(self, name):
        def _op(*a, **k):
            return _MockInst()
        return _op


SEM_NAMES = (
    "c_ld",
    "vg_l0", "vg_l1", "vu_l0", "vu_l1", "vd_l0", "vd_l1",
    "vg_s0", "vg_s1", "vu_s0", "vu_s1", "vd_s0", "vd_s1",
    "f_x", "f_w0", "f_w1", "f_d0", "f_d1", "f_o0", "f_o1", "f_r0", "f_r1",
    "q_x0", "q_x1", "q_w0", "q_w1", "q_s0", "q_s1",
    "a_l0", "a_l1", "a_st", "o_ld", "o_x0", "o_x1", "o_s0", "o_s1",
    "cc1g0", "cc1g1", "cc1g2", "cc1g3",
    "cc2g0", "cc2g1", "cc2g2", "cc2g3", "o_t0", "o_t1", "o_g2",
    "e_mm", "e_ve", "e_act", "e_po",
)


def phase_const(S_, C, ev):
    def sync(eng):
        for src, dst in C["const_pairs"]:
            S_["c_ld"].inc(eng.dma_start(out=dst[:, :], in_=src), 16)

    def vector(eng):
        eng.wait_ge(S_["c_ld"].h, S_["c_ld"].n)
        S_["e_ve"].inc(eng.tensor_scalar_mul(
            out=C["sdh_sb"][:, :], in0=C["sdh_sb"][:, :], scalar1=0.5))

    return {"sync": sync, "vector": vector}


def phase_convert(S_, C, ev):
    wg8, wu8, wd8 = C["wg8"], C["wu8"], C["wd8"]
    wg_bf, wu_bf, wd_bf = C["wg_bf"], C["wu_bf"], C["wd_bf"]
    bg8, bgb = C["cv_g8"], C["cv_gb"]
    bu8, bub = C["cv_u8"], C["cv_ub"]
    bd8, bdb = C["cv_d8"], C["cv_db"]
    sve, spo = S_["e_ve"], S_["e_po"]

    def sync(eng):
        for ic in range(NIC):
            p = ic % 2
            if ic >= 2:
                eng.wait_ge(sve.h, ev[("cv_g", ic - 2)])
                eng.wait_ge(spo.h, ev[("cv_u", ic - 2)])
            S_[f"vg_l{p}"].inc(eng.dma_start(
                out=bg8[p][:, :], in_=wg8[ic * 128:(ic + 1) * 128, :]), 16)
            ev[("g_ld", ic)] = S_[f"vg_l{p}"].n
            S_[f"vu_l{p}"].inc(eng.dma_start(
                out=bu8[p][:, :], in_=wu8[ic * 128:(ic + 1) * 128, :]), 16)
            ev[("u_ld", ic)] = S_[f"vu_l{p}"].n
        for oc in range(NOC):
            p = oc % 2
            if oc >= 2:
                s = sve if (oc - 2) % 2 == 0 else spo
                eng.wait_ge(s.h, ev[("cv_d", oc - 2)])
            S_[f"vd_l{p}"].inc(eng.dma_start(
                out=bd8[p][:, :], in_=wd8[oc * 128:(oc + 1) * 128, :]), 16)
            ev[("d_ld", oc)] = S_[f"vd_l{p}"].n

    def vector(eng):
        for ic in range(NIC):
            p = ic % 2
            eng.wait_ge(S_[f"vg_l{p}"].h, ev[("g_ld", ic)])
            if ic >= 2:
                eng.wait_ge(S_[f"vg_s{p}"].h, ev[("g_st", ic - 2)])
            ev[("cv_g", ic)] = sve.inc(
                eng.tensor_copy(out=bgb[p][:, :], in_=bg8[p][:, :]))
        for oc in range(0, NOC, 2):
            p = 0
            eng.wait_ge(S_[f"vd_l{p}"].h, ev[("d_ld", oc)])
            if oc >= 2:
                eng.wait_ge(S_[f"vd_s{p}"].h, ev[("d_st", oc - 2)])
            ev[("cv_d", oc)] = sve.inc(
                eng.tensor_copy(out=bdb[p][:, :], in_=bd8[p][:, :]))

    def gpsimd(eng):
        for ic in range(NIC):
            p = ic % 2
            eng.wait_ge(S_[f"vu_l{p}"].h, ev[("u_ld", ic)])
            if ic >= 2:
                eng.wait_ge(S_[f"vu_s{p}"].h, ev[("u_st", ic - 2)])
            ev[("cv_u", ic)] = spo.inc(
                eng.tensor_copy(out=bub[p][:, :], in_=bu8[p][:, :]))
        for oc in range(1, NOC, 2):
            p = 1
            eng.wait_ge(S_[f"vd_l{p}"].h, ev[("d_ld", oc)])
            if oc >= 2:
                eng.wait_ge(S_[f"vd_s{p}"].h, ev[("d_st", oc - 2)])
            ev[("cv_d", oc)] = spo.inc(
                eng.tensor_copy(out=bdb[p][:, :], in_=bd8[p][:, :]))

    def scalar(eng):
        for ic in range(NIC):
            p = ic % 2
            eng.wait_ge(sve.h, ev[("cv_g", ic)])
            ev[("g_st", ic)] = S_[f"vg_s{p}"].inc(eng.dma_start(
                out=wg_bf[ic * 128:(ic + 1) * 128, :], in_=bgb[p][:, :]), 16)
            eng.wait_ge(spo.h, ev[("cv_u", ic)])
            ev[("u_st", ic)] = S_[f"vu_s{p}"].inc(eng.dma_start(
                out=wu_bf[ic * 128:(ic + 1) * 128, :], in_=bub[p][:, :]), 16)
        for oc in range(NOC):
            p = oc % 2
            s = sve if oc % 2 == 0 else spo
            eng.wait_ge(s.h, ev[("cv_d", oc)])
            ev[("d_st", oc)] = S_[f"vd_s{p}"].inc(eng.dma_start(
                out=wd_bf[oc * 128:(oc + 1) * 128, :], in_=bdb[p][:, :]), 16)

    return {"sync": sync, "vector": vector, "gpsimd": gpsimd, "scalar": scalar}


def phase_ffn(S_, C, ev, src_d, dst_d, tag, do_conv=False):
    ones_sb = C["ones_sb"]; sg_sb = C["sg_sb"]; su_sb = C["su_sb"]; sdh_sb = C["sdh_sb"]
    xf = C["ffn_xf"]; xnb = C["ffn_xn2"]; hb = C["ffn_h"]
    xrb = C["ffn_xr"]
    wgb = C["ffn_wg"]; wub = C["ffn_wu"]; wdb = C["ffn_wd"]
    gsb = C["ffn_g"]; sqb = C["ffn_sq"]; xbb = C["ffn_xb"]; lntmp = C["ffn_lnt"]
    m_sb = C["ffn_m"]; v_sb = C["ffn_v"]; a_sb = C["ffn_A"]; b_sb = C["ffn_B"]
    x1b = C["ffn_x1"]
    ps_sum = C["PS"][0]; ps_sq = C["PS"][1]
    ps_g = C["PS"][2]; ps_u = C["PS"][3]; ps_d = [C["PS"][4], C["PS"][5]]
    wg_bf = C["wg_bf"]; wu_bf = C["wu_bf"]; wd_bf = C["wd_bf"]

    if do_conv:
        wg8_in, wu8_in, wd8_in = C["wg8"], C["wu8"], C["wd8"]
        stg = wdb[0].bitcast(I8)
        stg_g8 = [stg[:, 0:2048], stg[:, 2048:4096]]
        stg_u8 = [stg[:, 4096:6144], stg[:, 6144:8192]]
        stg_d8 = [stg[:, 8192:12288], stg[:, 12288:16384]]
        cvd_out = [wdb[1][:, 0:4096], wdb[1][:, 4096:8192]]
    sx = S_["f_x"]
    srp = [S_["f_r0"], S_["f_r1"]]
    swp = [S_["f_w0"], S_["f_w1"]]
    sdp = [S_["f_d0"], S_["f_d1"]]
    smm, sve, sact = S_["e_mm"], S_["e_ve"], S_["e_act"]
    sop = [S_["f_o0"], S_["f_o1"]]

    def K(*a):
        return (tag,) + a

    def sync(eng):
        def xld(tt):
            if tt > 0:
                eng.wait_ge(sve.h, ev[K("xn", tt - 1)])
            srcx = src_d[:, tt * TT:(tt + 1) * TT].rearrange(
                "(dt p) t -> p dt t", p=128)
            dst3 = xf[:, :].rearrange("p (dt t) -> p dt t", dt=NDT)
            sx.inc(eng.dma_start(out=dst3, in_=srcx), 16)
            ev[K("xld", tt)] = sx.n

        def wld(tt):
            if tt == 0 and do_conv:
                for ic in range(NIC):
                    p = ic % 2
                    if ic >= 2:
                        eng.wait_ge(sve.h, ev[K("cvg", ic - 2)])
                        eng.wait_ge(S_["e_po"].h, ev[K("cvu", ic - 2)])
                    S_[f"vg_l{p}"].inc(eng.dma_start(
                        out=stg_g8[p][:, :],
                        in_=wg8_in[ic * 128:(ic + 1) * 128, :]), 16)
                    ev[K("g8ld", ic)] = S_[f"vg_l{p}"].n
                    S_[f"vu_l{p}"].inc(eng.dma_start(
                        out=stg_u8[p][:, :],
                        in_=wu8_in[ic * 128:(ic + 1) * 128, :]), 16)
                    ev[K("u8ld", ic)] = S_[f"vu_l{p}"].n
                    if ic % 2 == 1:
                        c = ic // 2
                        p2 = c % 2
                        if c >= 2:
                            eng.wait_ge(S_["e_act"].h, ev[K("cvd", c - 2)])
                        oc, half = c // 2, c % 2
                        S_[f"vd_l{p2}"].inc(eng.dma_start(
                            out=stg_d8[p2][:, :],
                            in_=wd8_in[oc * 128:(oc + 1) * 128,
                                       half * 4096:(half + 1) * 4096]), 16)
                        ev[K("d8ld", c)] = S_[f"vd_l{p2}"].n
                return
            for ic in range(NIC):
                g = tt * NIC + ic
                p = g % 2
                if g >= 2:
                    eng.wait_ge(smm.h, ev[K("guw", g - 2)])
                if tt == 1 and do_conv and ic < 2:
                    eng.wait_ge(S_[f"vg_s{p}"].h, ev[K("gst", NIC - 2 + p)])
                    eng.wait_ge(S_[f"vu_s{p}"].h, ev[K("ust", NIC - 2 + p)])
                swp[p].inc(eng.dma_start(
                    out=wgb[p][:, :], in_=wg_bf[ic * 128:(ic + 1) * 128, :]), 16)
                swp[p].inc(eng.dma_start(
                    out=wub[p][:, :], in_=wu_bf[ic * 128:(ic + 1) * 128, :]), 16)
                ev[K("wld", g)] = swp[p].n

        def dld(tt):
            for oc in range(NOC):
                g = tt * NOC + oc
                p = g % 2
                if tt == 0 and do_conv and oc < 2:
                    eng.wait_ge(sve.h, ev[K("cvg_last")])
                    eng.wait_ge(S_["e_po"].h, ev[K("cvu_last")])
                    eng.wait_ge(S_["vd_s0"].h, ev[K("dst_last0")])
                    eng.wait_ge(S_["vd_s1"].h, ev[K("dst_last1")])
                if g >= 2:
                    eng.wait_ge(smm.h, ev[K("dmm", g - 2)])
                sdp[p].inc(eng.dma_start(
                    out=wdb[p][:, :], in_=wd_bf[oc * 128:(oc + 1) * 128, :]), 16)
                ev[K("dld", g)] = sdp[p].n

        xld(0)
        for tt in range(NT):
            wld(tt)
            if tt + 1 < NT:
                xld(tt + 1)
            dld(tt)

    def tensor(eng):
        def stats(tt):
            for dt in range(NDT):
                p = dt % 2
                eng.wait_ge(sve.h, ev[K("cast", tt, dt)])
                if dt == 0 and tt > 0:
                    eng.wait_ge(sve.h, ev[K("stats", tt - 1)])
                eng.matmul(ps_sum[:, :], ones_sb[:, :], xbb[p][:, :],
                           start=(dt == 0), stop=(dt == NDT - 1))
                mmi = eng.matmul(ps_sq[:, :], ones_sb[:, :], sqb[p][:, :],
                                 start=(dt == 0), stop=(dt == NDT - 1))
                ev[K("statp", tt, dt)] = smm.inc(mmi)

        def gateup(tt):
            xn = xnb[tt % 2]
            for ic in range(NIC):
                g = tt * NIC + ic
                p = g % 2
                if tt == 0 and do_conv:
                    eng.wait_ge(sve.h, ev[K("cvg", ic)])
                    eng.wait_ge(S_["e_po"].h, ev[K("cvu", ic)])
                else:
                    eng.wait_ge(swp[p].h, ev[K("wld", g)])
                if ic == 0:
                    eng.wait_ge(sve.h, ev[K("xn", tt)])
                if g > 0:
                    eng.wait_ge(sact.h, ev[K("silu", g - 1)])
                for dt in range(NDT):
                    mmi = eng.matmul(ps_g[:, :], wgb[p][:, dt * 128:(dt + 1) * 128],
                                     xn[:, dt * TT:(dt + 1) * TT],
                                     start=(dt == 0), stop=(dt == NDT - 1))
                ev[K("gmm", g)] = smm.inc(mmi)
                if g > 0:
                    eng.wait_ge(sve.h, ev[K("h", g - 1)])
                for dt in range(NDT):
                    mmi = eng.matmul(ps_u[:, :], wub[p][:, dt * 128:(dt + 1) * 128],
                                     xn[:, dt * TT:(dt + 1) * TT],
                                     start=(dt == 0), stop=(dt == NDT - 1))
                ev[K("guw", g)] = smm.inc(mmi)

        def down(tt):
            for oc in range(NOC):
                g = tt * NOC + oc
                p = g % 2
                eng.wait_ge(sdp[p].h, ev[K("dld", g)])
                if oc == 0:
                    eng.wait_ge(sve.h, ev[K("h", tt * NIC + NIC - 1)])
                if g >= 2:
                    eng.wait_ge(sve.h, ev[K("x1ve", (g - 2) // NOC, (g - 2) % NOC)])
                for ic in range(NIC):
                    mmi = eng.matmul(ps_d[p][:, :],
                                     wdb[p][:, ic * 128:(ic + 1) * 128],
                                     hb[:, ic * TT:(ic + 1) * TT],
                                     start=(ic == 0), stop=(ic == NIC - 1))
                ev[K("dmm", g)] = smm.inc(mmi)

        stats(0)
        for tt in range(NT):
            gateup(tt)
            if tt + 1 < NT:
                stats(tt + 1)
            down(tt)

    def vector(eng):
        def ln(tt):
            xn = xnb[tt % 2]
            for dt in range(NDT):
                p = dt % 2
                eng.wait_ge(sx.h, ev[K("xld", tt)])
                if dt >= 2 or tt > 0:
                    e = (ev[K("statp", tt, dt - 2)] if dt >= 2 else
                         ev[K("statp", tt - 1, NDT - 2 + dt)])
                    eng.wait_ge(smm.h, e)
                xft = xf[:, dt * TT:(dt + 1) * TT]
                sve.inc(eng.tensor_copy(out=xbb[p][:, :], in_=xft))
                ev[K("cast", tt, dt)] = sve.inc(
                    eng.tensor_mul(out=sqb[p][:, :], in0=xft, in1=xft))
            eng.wait_ge(smm.h, ev[K("statp", tt, NDT - 1)])
            sve.inc(eng.tensor_scalar_mul(out=m_sb[:, :], in0=ps_sum[:, :],
                                          scalar1=float(1.0 / D)))
            eng.wait_ge(sve.h, sve.n)
            sve.inc(eng.scalar_tensor_tensor(out=v_sb[:, :], in0=m_sb[:, :],
                                             scalar=-1.0, in1=m_sb[:, :],
                                             op0=OP.mult, op1=OP.mult))
            eng.wait_ge(sve.h, sve.n)
            ev[K("stats", tt)] = sve.inc(
                eng.scalar_tensor_tensor(out=v_sb[:, :], in0=ps_sq[:, :],
                                         scalar=float(1.0 / D), in1=v_sb[:, :],
                                         op0=OP.mult, op1=OP.add))
            eng.wait_ge(sact.h, ev[K("rstd", tt)])
            sve.inc(eng.scalar_tensor_tensor(out=b_sb[:, :], in0=m_sb[:, :],
                                             scalar=-1.0, in1=a_sb[:, :],
                                             op0=OP.mult, op1=OP.mult))
            eng.wait_ge(sve.h, sve.n)
            for dt in range(NDT):
                xft = xf[:, dt * TT:(dt + 1) * TT]
                xnt = xn[:, dt * TT:(dt + 1) * TT]
                if tt >= 2 and dt == 0:
                    eng.wait_ge(smm.h, ev[K("guw", (tt - 2) * NIC + NIC - 1)])
                sve.inc(eng.tensor_mul(out=lntmp[:, :], in0=xft, in1=a_sb[:, :]))
                eng.wait_ge(sve.h, sve.n)
                sve.inc(eng.tensor_add(out=xnt, in0=lntmp[:, :], in1=b_sb[:, :]))
                eng.wait_ge(sve.h, sve.n)
            ev[K("xn", tt)] = sve.n

        def hp1(tt, ic):
            g = tt * NIC + ic
            eng.wait_ge(sact.h, ev[K("silu", g)])
            eng.wait_ge(smm.h, ev[K("guw", g)])
            ev[K("h", g)] = sve.inc(eng.scalar_tensor_tensor(
                out=hb[:, ic * TT:(ic + 1) * TT], in0=ps_u[:, :],
                scalar=su_sb[:, ic:ic + 1], in1=gsb[g % 2][:, :],
                op0=OP.mult, op1=OP.mult))

        def hpass(tt):
            for ic in range(NIC):
                hp1(tt, ic)

        def x1pass(tt):
            for oc in range(NOC):
                g = tt * NOC + oc
                p = g % 2
                eng.wait_ge(smm.h, ev[K("dmm", g)])
                eng.wait_ge(srp[p].h, ev[K("xrld", g)])
                if g >= 2:
                    eng.wait_ge(sop[p].h, ev[K("x1st", g - 2)])
                ev[K("x1ve", tt, oc)] = sve.inc(eng.scalar_tensor_tensor(
                    out=x1b[p][:, :], in0=ps_d[p][:, :],
                    scalar=sdh_sb[:, oc:oc + 1],
                    in1=xrb[p][:, :],
                    op0=OP.mult, op1=OP.add))

        def convg(ic):
            p = ic % 2
            eng.wait_ge(S_[f"vg_l{p}"].h, ev[K("g8ld", ic)])
            if ic >= 2:
                eng.wait_ge(smm.h, ev[K("guw", ic - 2)])
                eng.wait_ge(S_[f"vg_s{p}"].h, ev[K("gst", ic - 2)])
            ev[K("cvg", ic)] = sve.inc(
                eng.tensor_copy(out=wgb[p][:, :], in_=stg_g8[p][:, :]))

        ln(0)
        for tt in range(NT):
            if tt == 0 and do_conv:
                for ic in range(NIC):
                    convg(ic)
                    if ic >= 1:
                        hp1(tt, ic - 1)
                hp1(tt, NIC - 1)
                ev[K("cvg_last")] = ev[K("cvg", NIC - 1)]
            else:
                hpass(tt)
            if tt + 1 < NT:
                ln(tt + 1)
            x1pass(tt)

    def scalar(eng):
        def rstd(tt):
            eng.wait_ge(sve.h, ev[K("stats", tt)])
            sact.inc(eng.activation(out=a_sb[:, :], in_=v_sb[:, :], func=AF.Ln,
                                    bias=np.float32(1e-5)))
            eng.wait_ge(sact.h, sact.n)
            ev[K("rstd", tt)] = sact.inc(
                eng.activation(out=a_sb[:, :], in_=a_sb[:, :], func=AF.Exp,
                               scale=np.float32(-0.5)))

        def silu(tt):
            for ic in range(NIC):
                g = tt * NIC + ic
                eng.wait_ge(smm.h, ev[K("gmm", g)])
                if g >= 2:
                    eng.wait_ge(sve.h, ev[K("h", g - 2)])
                ev[K("silu", g)] = sact.inc(eng.activation(
                    out=gsb[g % 2][:, :], in_=ps_g[:, :],
                    func=AF.Silu, scale=sg_sb[:, ic:ic + 1]))

        def stores(tt):
            for oc in range(NOC):
                g = tt * NOC + oc
                p = g % 2
                if g >= 2:
                    eng.wait_ge(sve.h, ev[K("x1ve", (g - 2) // NOC, (g - 2) % NOC)])
                srp[p].inc(eng.dma_start(
                    out=xrb[p][:, :],
                    in_=src_d[oc * 128:(oc + 1) * 128, tt * TT:(tt + 1) * TT]), 16)
                ev[K("xrld", g)] = srp[p].n
                eng.wait_ge(sve.h, ev[K("x1ve", tt, oc)])
                ev[K("x1st", g)] = sop[p].inc(eng.dma_start(
                    out=dst_d[oc * 128:(oc + 1) * 128, tt * TT:(tt + 1) * TT],
                    in_=x1b[p][:, :]), 16)

        def silu1(tt, ic):
            g = tt * NIC + ic
            eng.wait_ge(smm.h, ev[K("gmm", g)])
            if g >= 2:
                eng.wait_ge(sve.h, ev[K("h", g - 2)])
            ev[K("silu", g)] = sact.inc(eng.activation(
                out=gsb[g % 2][:, :], in_=ps_g[:, :],
                func=AF.Silu, scale=sg_sb[:, ic:ic + 1]))

        rstd(0)
        for tt in range(NT):
            if tt == 0 and do_conv:
                for ic in range(NIC):
                    p = ic % 2
                    eng.wait_ge(sve.h, ev[K("cvg", ic)])
                    ev[K("gst", ic)] = S_[f"vg_s{p}"].inc(eng.dma_start(
                        out=wg_bf[ic * 128:(ic + 1) * 128, :],
                        in_=wgb[p][:, :]), 16)
                    eng.wait_ge(S_["e_po"].h, ev[K("cvu", ic)])
                    ev[K("ust", ic)] = S_[f"vu_s{p}"].inc(eng.dma_start(
                        out=wu_bf[ic * 128:(ic + 1) * 128, :],
                        in_=wub[p][:, :]), 16)
                    silu1(tt, ic)
                    if ic % 2 == 1 and ic >= 3:
                        c = ic // 2 - 1
                        p2 = c % 2
                        eng.wait_ge(S_[f"vd_l{p2}"].h, ev[K("d8ld", c)])
                        if c >= 2:
                            eng.wait_ge(S_[f"vd_s{p2}"].h, ev[K("dst", c - 2)])
                        ev[K("cvd", c)] = sact.inc(eng.activation(
                            out=cvd_out[p2][:, :], in_=stg_d8[p2][:, :],
                            func=AF.Copy))
                        eng.wait_ge(sact.h, sact.n)
                        oc, half = c // 2, c % 2
                        ev[K("dst", c)] = S_[f"vd_s{p2}"].inc(eng.dma_start(
                            out=wd_bf[oc * 128:(oc + 1) * 128,
                                      half * 4096:(half + 1) * 4096],
                            in_=cvd_out[p2][:, :]), 16)
                for c in (31,):
                    p2 = c % 2
                    eng.wait_ge(S_[f"vd_l{p2}"].h, ev[K("d8ld", c)])
                    eng.wait_ge(S_[f"vd_s{p2}"].h, ev[K("dst", c - 2)])
                    ev[K("cvd", c)] = sact.inc(eng.activation(
                        out=cvd_out[p2][:, :], in_=stg_d8[p2][:, :],
                        func=AF.Copy))
                    eng.wait_ge(sact.h, sact.n)
                    oc, half = c // 2, c % 2
                    ev[K("dst", c)] = S_[f"vd_s{p2}"].inc(eng.dma_start(
                        out=wd_bf[oc * 128:(oc + 1) * 128,
                                  half * 4096:(half + 1) * 4096],
                        in_=cvd_out[p2][:, :]), 16)
                ev[K("dst_last0")] = S_["vd_s0"].n
                ev[K("dst_last1")] = S_["vd_s1"].n
            else:
                silu(tt)
            if tt + 1 < NT:
                rstd(tt + 1)
            stores(tt)

    def gpsimd(eng):
        if not do_conv:
            return
        for ic in range(NIC):
            p = ic % 2
            eng.wait_ge(S_[f"vu_l{p}"].h, ev[K("u8ld", ic)])
            if ic >= 2:
                eng.wait_ge(smm.h, ev[K("guw", ic - 2)])
                eng.wait_ge(S_[f"vu_s{p}"].h, ev[K("ust", ic - 2)])
            ev[K("cvu", ic)] = S_["e_po"].inc(
                eng.tensor_copy(out=wub[p][:, :], in_=stg_u8[p][:, :]))
        ev[K("cvu_last")] = ev[K("cvu", NIC - 1)]

    return {"sync": sync, "tensor": tensor, "vector": vector, "scalar": scalar,
            "gpsimd": gpsimd}


def phase_qkv(S_, C, ev):
    ones_sb = C["ones_sb"]
    x1d = C["x1d"]; wpT = C["wpT"]; qkv_con = C["qkv_con"]
    xc = C["q_xc"]
    sqb = C["q_sq"]; xbb = C["q_xb"]
    m_sb = C["q_m"]; v_sb = C["q_v"]; a_sb = C["q_A"]; b_sb = C["q_B"]
    lntmp = C["q_lnt"]
    xn1 = C["q_xn"]
    wpf = C["q_wpf"]
    wpb = C["q_wpb"]
    ob = C["q_ob"]
    ps_sum2 = [C["PS"][0], C["PS"][2]]
    ps_sq2 = [C["PS"][1], C["PS"][3]]
    ps_q = [C["PS"][4], C["PS"][5]]
    sxp = [S_["q_x0"], S_["q_x1"]]
    swp = [S_["q_w0"], S_["q_w1"]]
    ssp = [S_["q_s0"], S_["q_s1"]]
    smm, sve, sact = S_["e_mm"], S_["e_ve"], S_["e_act"]

    LSEQ = []
    for th in range(4):
        LSEQ += [(0, th, dt) for dt in range(NDT)]
        LSEQ += [(1, th, dt) for dt in range(NDT)]

    OCSEQ = []
    OCROW = {}
    for g in range(4):
        for sec in range(3):
            for slot in range(4):
                oc = sec * 16 + slot * 4 + g
                OCSEQ.append(oc)
                OCROW[oc] = g * 1536 + sec * 512 + slot * 128

    def sync(eng):
        def wqld(k):
            oc = OCSEQ[k]
            p = k % 2
            if k >= 2:
                eng.wait_ge(S_["e_po"].h, ev[("qcv", k - 2)])
            swp[p].inc(eng.dma_start(
                out=wpf[p][:, :], in_=wpT[oc * 128:(oc + 1) * 128, :]), 16)
            ev[("qwld", k)] = swp[p].n

        for j, key in enumerate(LSEQ):
            rnd, th, dt = key
            p = j % 2
            if j >= 2:
                eng.wait_ge(sve.h, ev[("qx",) + LSEQ[j - 2]])
            sxp[p].inc(eng.dma_start(
                out=xc[p][:, :],
                in_=x1d[dt * 128:(dt + 1) * 128, th * TT:(th + 1) * TT]), 16)
            ev[("qxld",) + key] = sxp[p].n
            if j == 31:
                for k in range(6):
                    wqld(k)
        for k in range(6, 48):
            wqld(k)

    def vector(eng):
        jof = {key: j for j, key in enumerate(LSEQ)}
        for th in range(4):
            for dt in range(NDT):
                j = jof[(0, th, dt)]
                p = j % 2
                eng.wait_ge(sxp[p].h, ev[("qxld", 0, th, dt)])
                i = th * NDT + dt
                if i >= 2:
                    eng.wait_ge(smm.h, ev[("qstat", i - 2)])
                sve.inc(eng.tensor_copy(out=xbb[p][:, :], in_=xc[p][:, :]))
                ev[("qx", 0, th, dt)] = sve.inc(
                    eng.tensor_mul(out=sqb[p][:, :], in0=xc[p][:, :], in1=xc[p][:, :]))
            eng.wait_ge(smm.h, ev[("qstat", th * NDT + NDT - 1)])
            sve.inc(eng.tensor_scalar_mul(out=m_sb[:, :], in0=ps_sum2[th % 2][:, :],
                                          scalar1=float(1.0 / D)))
            eng.wait_ge(sve.h, sve.n)
            sve.inc(eng.scalar_tensor_tensor(out=v_sb[:, :], in0=m_sb[:, :],
                                             scalar=-1.0, in1=m_sb[:, :],
                                             op0=OP.mult, op1=OP.mult))
            eng.wait_ge(sve.h, sve.n)
            ev[("qvar", th)] = sve.inc(
                eng.scalar_tensor_tensor(out=v_sb[:, :], in0=ps_sq2[th % 2][:, :],
                                         scalar=float(1.0 / D), in1=v_sb[:, :],
                                         op0=OP.mult, op1=OP.add))
            eng.wait_ge(sact.h, ev[("qrstd", th)])
            sve.inc(eng.scalar_tensor_tensor(out=b_sb[:, :], in0=m_sb[:, :],
                                             scalar=-1.0, in1=a_sb[:, :],
                                             op0=OP.mult, op1=OP.mult))
            eng.wait_ge(sve.h, sve.n)
            for dt in range(NDT):
                j = jof[(1, th, dt)]
                p = j % 2
                eng.wait_ge(sxp[p].h, ev[("qxld", 1, th, dt)])
                xnt = xn1[:, dt * T + th * TT: dt * T + (th + 1) * TT]
                sve.inc(eng.tensor_mul(out=lntmp[:, :], in0=xc[p][:, :], in1=a_sb[:, :]))
                eng.wait_ge(sve.h, sve.n)
                ev[("qx", 1, th, dt)] = sve.inc(
                    eng.tensor_add(out=xnt, in0=lntmp[:, :], in1=b_sb[:, :]))
                eng.wait_ge(sve.h, sve.n)
            ev[("qxnth", th)] = sve.n
        ev[("qxn",)] = sve.n

    def tensor(eng):
        for th in range(4):
            for dt in range(NDT):
                i = th * NDT + dt
                p = (th * 32 + dt) % 2
                eng.wait_ge(sve.h, ev[("qx", 0, th, dt)])
                if dt == 0 and th > 1:
                    eng.wait_ge(sve.h, ev[("qvar", th - 2)])
                eng.matmul(ps_sum2[th % 2][:, :], ones_sb[:, :], xbb[p][:, :],
                           start=(dt == 0), stop=(dt == NDT - 1))
                mmi = eng.matmul(ps_sq2[th % 2][:, :], ones_sb[:, :], sqb[p][:, :],
                                 start=(dt == 0), stop=(dt == NDT - 1))
                ev[("qstat", i)] = smm.inc(mmi)
        SCHED = [(k, th) for th in range(4) for k in range(6)] + \
                [(k, th) for k in range(6, 48) for th in range(4)]
        C["q_sched"] = SCHED
        for i, (k, th) in enumerate(SCHED):
            wp = wpb[k % 6]
            eng.wait_ge(S_["e_po"].h, ev[("qcv", k)])
            pp = i % 2
            eng.wait_ge(sve.h, ev[("qxnth", th)])
            if i >= 2:
                eng.wait_ge(sact.h, ev[("qev",) + SCHED[i - 2]])
            for dt in range(NDT):
                mmi = eng.matmul(ps_q[pp][:, :], wp[:, dt * 128:(dt + 1) * 128],
                                 xn1[:, dt * T + th * TT: dt * T + (th + 1) * TT],
                                 start=(dt == 0), stop=(dt == NDT - 1))
            ev[("qmm", k, th)] = smm.inc(mmi)

    def scalar(eng):
        for th in range(4):
            eng.wait_ge(sve.h, ev[("qvar", th)])
            sact.inc(eng.activation(out=a_sb[:, :], in_=v_sb[:, :], func=AF.Ln,
                                    bias=np.float32(1e-5)))
            eng.wait_ge(sact.h, sact.n)
            ev[("qrstd", th)] = sact.inc(
                eng.activation(out=a_sb[:, :], in_=a_sb[:, :], func=AF.Exp,
                               scale=np.float32(-0.5)))
        SCHED = C["q_sched"]
        done_k = {}
        for i, (k, th) in enumerate(SCHED):
            oc = OCSEQ[k]
            row = OCROW[oc]
            pp = i % 2
            eng.wait_ge(smm.h, ev[("qmm", k, th)])
            if i >= 2:
                eng.wait_ge(ssp[pp].h, ev[("qst", i - 2)])
            scale = np.float32(QSCALE) if oc < 16 else np.float32(1.0)
            ev[("qev", k, th)] = sact.inc(eng.activation(
                out=ob[pp][:, :], in_=ps_q[pp][:, :], func=AF.Copy, scale=scale))
            eng.wait_ge(sact.h, sact.n)
            ev[("qst", i)] = ssp[pp].inc(eng.dma_start(
                out=qkv_con[row:row + 128, th * TT:(th + 1) * TT],
                in_=ob[pp][:, :]), 16)
            done_k[k] = done_k.get(k, 0) + 1
            if done_k[k] == 4 and all(done_k.get(kk, 0) == 4
                                      for kk in range(12 * (k // 12), k + 1)) \
               and (k % 12 == 11 or False):
                g = k // 12
                eng.wait_ge(ssp[0].h, ssp[0].n)
                eng.wait_ge(ssp[1].h, ssp[1].n)
                ev[("qgdone", g)] = sact.inc(eng.activation(
                    out=C["q_dum"][:, :], in_=C["q_dum"][:, :], func=AF.Copy))

    def gpsimd(eng):
        SCHED4 = [(k, th) for th in range(4) for k in range(6)] + \
                 [(k, th) for k in range(6, 48) for th in range(4)]
        lastpos = {}
        for i_, kt_ in enumerate(SCHED4):
            lastpos[kt_[0]] = kt_
        spo = S_["e_po"]
        for k in range(48):
            p = k % 2
            eng.wait_ge(swp[p].h, ev[("qwld", k)])
            if k >= 6:
                eng.wait_ge(smm.h, ev[("qmm",) + lastpos[k - 6]])
            ev[("qcv", k)] = spo.inc(
                eng.tensor_copy(out=wpb[k % 6][:, :], in_=wpf[p][:, :]))
        for g in range(4):
            eng.wait_ge(sact.h, ev[("qgdone", g)])
            S_[f"cc1g{g}"].inc(eng.collective_compute(
                "AllGather", OP.bypass, replica_groups=[[0, 1, 2, 3], [4, 5, 6, 7]],
                ins=[qkv_con.ap()[g * 1536:(g + 1) * 1536, :].opt()],
                outs=[C["qkv_gat"].ap()[g * 6144:(g + 1) * 6144, :].opt()]))

    return {"sync": sync, "vector": vector, "tensor": tensor, "scalar": scalar,
            "gpsimd": gpsimd}


def phase_cc(S_, C, ev, which):
    groups = [[0, 1, 2, 3], [4, 5, 6, 7]]

    def gpsimd(eng):
        if which == 1:
            cin, cout, sem = C["qkv_con"], C["qkv_gat"], S_["cc1"]
        else:
            cin, cout, sem = C["o_con"], C["o_gat"], S_["cc2"]
        sem.inc(eng.collective_compute(
            "AllGather", OP.bypass, replica_groups=groups,
            ins=[cin.ap().opt()], outs=[cout.ap().opt()]))
        eng.wait_ge(sem.h, sem.n)

    return {"gpsimd": gpsimd}


def phase_attn(S_, C, ev):
    ident_sb = C["ident_sb"]; mask_sb = C["mask_sb"]
    qkv_gat = C["qkv_gat"]; o_con = C["o_con"]
    ones_sb = C["ones_sb"]
    q2 = C["at_q"]; k2 = C["at_k"]; vf2 = C["at_vf"]; vt2 = C["at_vt"]
    pstage = C["at_ps"]
    scr = C["at_scr"]
    rsb = C["at_rs"]
    o_acc = C["at_o"]
    ps_s4 = C["PS"][0:4]
    ps_sm = [C["PS"][4], C["PS"][5]]
    ps_o = [C["PS"][6], C["PS"][7]]
    sldp = [S_["a_l0"], S_["a_l1"]]
    sst = S_["a_st"]
    smm, sve, sact = S_["e_mm"], S_["e_ve"], S_["e_act"]

    def geom(p):
        nk2 = 2 * p + 2
        return nk2, nk2 // 2

    def pair_units(p):
        return [(p, g) for g in range(geom(p)[1])]

    def last_unit(p):
        return (p, geom(p)[1] - 1)

    LASTG = ("apvg", NH_LOC - 1, 31, geom(31)[1] - 1)

    def sync(eng):
        with eng.register("at_m") as rm, eng.register("at_off") as ro:
            pid = eng.partition_id()
            eng.reg_mov(rm, pid)
            eng.reg_mod(rm, rm, 4)
            eng.reg_mul(rm, rm, 128 * T)
            for hl in range(NH_LOC):
                par = hl % 2
                if hl > 1:
                    eng.wait_ge(smm.h, ev[("apv_last", hl - 2)])
                eng.wait_ge(S_[f"cc1g{hl}"].h, 1)
                for sec, dsts in ((0, q2), (1, k2), (2, vf2)):
                    base = (hl * 6144 + sec * 512) * T
                    eng.reg_add(ro, rm, base)
                    srcap = (bass.AP(qkv_gat, ro,
                                     [[T, 128], [1536 * T, 4], [1, T]])
                             if C["emit"] else None)
                    dst3 = dsts[par][:, :].rearrange("p (r t) -> p r t", r=4)
                    sldp[par].inc(eng.dma_start(out=dst3, in_=srcap), 16)
                ev[("ald", hl)] = sldp[par].n

    def tensor(eng):
        relsc = {}
        for hl in range(NH_LOC):
            par = hl % 2
            q_sb, k_sb, vf_sb, vt_sb = q2[par], k2[par], vf2[par], vt2[par]
            eng.wait_ge(sldp[par].h, ev[("ald", hl)])
            for vg in range(8):
                pb = vg % 4
                r = relsc.get(pb)
                if r is not None:
                    eng.wait_ge(r[0].h, ev[r[1]])
                for j in range(8):
                    kt = vg * 8 + j
                    mmi = eng.transpose(
                        ps_s4[pb].bitcast(BF16)[:, j * 128:(j + 1) * 128],
                        vf_sb[:, kt * 128:(kt + 1) * 128], ident_sb[:, :])
                ev[("avt", hl, vg)] = smm.inc(mmi)
                relsc[pb] = (sve, ("avte", hl, vg))

            units = []
            for p in range(32):
                units += pair_units(p)

            ndone = {}

            def consume(u):
                p, g = u
                nk2, ngrp = geom(p)
                pb = ev[("aunit_pb", hl, p, g)]
                eng.wait_ge(sact.h, ev[("aexp", hl, p, g)])
                if ndone.get(p, 0) == 0:
                    if p >= 2:
                        eng.wait_ge(sve.h, ev[("aoev", hl, p - 2)])
                    elif hl > 0:
                        eng.wait_ge(sve.h, ev[("aoev", hl - 1, 30 + p)])
                for j in range(2):
                    kt = g * 2 + j
                    pst = pstage[pb][:, j * 256:(j + 1) * 256]
                    nd_ = ndone[p] = ndone.get(p, 0) + 1
                    eng.matmul(ps_sm[p % 2][:, 0:256], ones_sb[:, :], pst,
                               start=(nd_ == 1), stop=(nd_ == nk2))
                    mmi = eng.matmul(ps_o[p % 2][:, 0:256],
                                     vt_sb[:, kt * 128:(kt + 1) * 128], pst,
                                     start=(nd_ == 1), stop=(nd_ == nk2))
                ev[("apvg", hl, p, g)] = smm.inc(mmi)

            pend = []
            for ui, u in enumerate(units):
                p, g = u
                pb = ui % 4
                ev[("aunit_pb", hl, p, g)] = pb
                qs = q_sb[:, 256 * p:256 * (p + 1)]
                r = relsc.get(pb)
                if r is not None:
                    eng.wait_ge(r[0].h, ev[r[1]])
                for j in range(2):
                    kt = g * 2 + j
                    mmi = eng.matmul(
                        ps_s4[pb][:, j * 256:(j + 1) * 256],
                        k_sb[:, kt * 128:(kt + 1) * 128], qs,
                        start=True, stop=True)
                ev[("asc", hl, p, g)] = smm.inc(mmi)
                relsc[pb] = (sact, ("aexp", hl, p, g))
                pend.append(u)
                if len(pend) > 2:
                    consume(pend.pop(0))
            while pend:
                consume(pend.pop(0))
            ev[("asc_last", hl)] = ev[("asc", hl) + last_unit(31)]
            ev[("apv_last", hl)] = ev[("apvg", hl) + last_unit(31)]

    def scalar(eng):
        pstrel = {}
        for hl in range(NH_LOC):
            for p in range(32):
                nk2, ngrp = geom(p)
                for (pp_, g) in pair_units(p):
                    pb = ev[("aunit_pb", hl, p, g)]
                    eng.wait_ge(smm.h, ev[("asc", hl, p, g)])
                    r = pstrel.get(pb)
                    if r is not None:
                        eng.wait_ge(smm.h, ev[r])
                    if g == ngrp - 1:
                        eng.wait_ge(sve.h, ev[("amask", hl, p)])
                        last = eng.activation(out=pstage[pb][:, :],
                                              in_=scr[:, :], func=AF.Exp)
                    else:
                        last = eng.activation(out=pstage[pb][:, :],
                                              in_=ps_s4[pb][:, :], func=AF.Exp)
                    ev[("aexp", hl, p, g)] = sact.inc(last)
                    pstrel[pb] = ("apvg", hl, p, g)
            eng.wait_ge(sve.h, ev[("aoev", hl, 31)])
            ev[("aost", hl)] = sst.inc(eng.dma_start(
                out=o_con[hl * 128:(hl + 1) * 128, :],
                in_=o_acc[:, hl * S:(hl + 1) * S]), 16)
            eng.wait_ge(sst.h, ev[("aost", hl)])
            ev[("agdone", hl)] = sact.inc(eng.activation(
                out=C["at_dum"][:, :], in_=C["at_dum"][:, :], func=AF.Copy))

    def gpsimd(eng):
        for hl in range(NH_LOC):
            eng.wait_ge(sact.h, ev[("agdone", hl)])
            S_[f"cc2g{hl}"].inc(eng.collective_compute(
                "AllGather", OP.bypass, replica_groups=[[0, 1, 2, 3], [4, 5, 6, 7]],
                ins=[o_con.ap()[hl * 128:(hl + 1) * 128, :].opt()],
                outs=[C["o_gat"].ap()[hl * 512:(hl + 1) * 512, :].opt()]))

    def vector(eng):
        for hl in range(NH_LOC):
            par = hl % 2
            vt_sb = vt2[par]
            for vg in range(8):
                pb = vg % 4
                eng.wait_ge(smm.h, ev[("avt", hl, vg)])
                if hl > 1 and vg == 0:
                    eng.wait_ge(smm.h, ev[("apv_last", hl - 2)])
                ev[("avte", hl, vg)] = sve.inc(eng.tensor_copy(
                    out=vt_sb[:, vg * 1024:(vg + 1) * 1024],
                    in_=ps_s4[pb].bitcast(BF16)[:, 0:1024]))
            for p in range(32):
                nk2, ngrp = geom(p)
                glast = ngrp - 1
                eng.wait_ge(smm.h, ev[("asc", hl, p, glast)])
                if p > 0 or hl > 0:
                    prevkey = (("aexp", hl, p - 1, geom(p - 1)[1] - 1) if p > 0
                               else ("aexp", hl - 1, 31, geom(31)[1] - 1))
                    eng.wait_ge(sact.h, ev[prevkey])
                pbl = ev[("aunit_pb", hl, p, glast)]
                ev[("amask", hl, p)] = sve.inc(eng.tensor_add(
                    out=scr[:, :], in0=ps_s4[pbl][:, :], in1=mask_sb[:, :]))
                lu = last_unit(p)
                eng.wait_ge(smm.h, ev[("apvg", hl) + lu])
                if p >= 2:
                    eng.wait_ge(sve.h, ev[("aoev", hl, p - 2)])
                elif hl > 0:
                    eng.wait_ge(sve.h, ev[("aoev", hl - 1, 30 + p)])
                sve.inc(eng.reciprocal(out=rsb[p % 2][:, :],
                                       in_=ps_sm[p % 2][:, 0:256]))
                eng.wait_ge(sve.h, sve.n)
                ev[("aoev", hl, p)] = sve.inc(eng.tensor_mul(
                    out=o_acc[:, (hl * 32 + p) * 256:(hl * 32 + p + 1) * 256],
                    in0=ps_o[p % 2][:, 0:256], in1=rsb[p % 2][:, :]))

    return {"sync": sync, "tensor": tensor, "scalar": scalar, "vector": vector,
            "gpsimd": gpsimd}


def phase_oproj(S_, C, ev):
    o_gat = C["o_gat"]; woT = C["woT"]; x1d = C["x1d"]; x2d = C["x2d"]
    o_sb = C["op_o"]
    wof = C["op_wf"]; wob = C["op_wb"]
    x1c = C["op_x1"]
    x2b = C["op_x2"]
    ps = [C["PS"][0], C["PS"][1]]
    solg = [S_["o_ld"], S_["q_x0"], S_["o_g2"], S_["a_l0"]]
    sxp = [S_["o_x0"], S_["o_x1"]]
    swp = [S_["q_w0"], S_["q_w1"]]
    ssp = [S_["o_s0"], S_["o_s1"]]
    smm, sve, sact = S_["e_mm"], S_["e_ve"], S_["e_act"]

    def oload(eng, hl, half=None):
        tag = f"{hl}{half}"
        with eng.register(f"op_m{tag}") as rm, eng.register(f"op_o{tag}") as ro:
            pid = eng.partition_id()
            eng.reg_mov(rm, pid)
            eng.reg_mod(rm, rm, 4)
            eng.reg_mul(rm, rm, T)
            eng.wait_ge(S_[f"cc2g{hl}"].h, 1)
            hof = 0 if half is None or half == 0 else T // 2
            eng.reg_add(ro, rm, hl * 512 * S + hof)
            tlen = T if half is None else T // 2
            srcap = (bass.AP(o_gat, ro,
                             [[S, 128], [128 * S, 4], [1, tlen]])
                     if C["emit"] else None)
            dst3 = o_sb[:, :].rearrange(
                "p (r f t) -> p r f t", r=4, f=4)[:, :, hl, :]
            if half is not None:
                dst3 = dst3[:, :, hof:hof + tlen]
            solg[hl].inc(eng.dma_start(out=dst3, in_=srcap), 16)
            ev[("oldg", hl, half)] = (hl, solg[hl].n)

    def sync(eng):
        oload(eng, 0)
        oload(eng, 3, half=0)
        for oc in range(NOC):
            p = oc % 2
            if oc >= 2:
                eng.wait_ge(sve.h, ev[("ocv", oc - 2)])
            swp[p].inc(eng.dma_start(
                out=wof[p][:, :], in_=woT[oc * 128:(oc + 1) * 128, :]), 16)
            ev[("owld", oc)] = swp[p].n

    def scalar(eng):
        oload(eng, 1)
        oload(eng, 3, half=1)
        for i in range(NOC * 4):
            oc, th = i // 4, i % 4
            p = i % 2
            if i >= 2:
                eng.wait_ge(sve.h, ev[("ox2", i - 2)])
            ssp[p].inc(eng.dma_start(
                out=x1c[p][:, :],
                in_=x1d[oc * 128:(oc + 1) * 128, th * TT:(th + 1) * TT]), 16)
            ev[("ox1ld", i)] = ssp[p].n

    def vector(eng):
        def conv(oc):
            p = oc % 2
            eng.wait_ge(swp[p].h, ev[("owld", oc)])
            if oc >= 2:
                eng.wait_ge(smm.h, ev[("omm", oc - 2, 3)])
            ev[("ocv", oc)] = sve.inc(
                eng.tensor_copy(out=wob[p][:, :], in_=wof[p][:, :]))

        conv(0)
        conv(1)
        for oc in range(NOC):
            for th in range(4):
                i = oc * 4 + th
                p = i % 2
                eng.wait_ge(smm.h, ev[("omm", oc, th)])
                eng.wait_ge(ssp[p].h, ev[("ox1ld", i)])
                if i >= 2:
                    eng.wait_ge(S_[f"o_t{(i - 2) % 2}"].h, ev[("ox2st", i - 2)])
                ev[("ox2", i)] = sve.inc(eng.tensor_add(
                    out=x2b[p][:, :], in0=ps[p][:, :], in1=x1c[p][:, :]))
            if oc + 2 < NOC:
                conv(oc + 2)

    def gpsimd(eng):
        oload(eng, 2)
        for i in range(NOC * 4):
            oc, th = i // 4, i % 4
            p = i % 2
            eng.wait_ge(sve.h, ev[("ox2", i)])
            ev[("ox2st", i)] = S_[f"o_t{p}"].inc(eng.dma_start(
                out=x2d[oc * 128:(oc + 1) * 128, th * TT:(th + 1) * TT],
                in_=x2b[p][:, :]), 16)

    DTSEQ = [g + 4 * slot for g in range(4) for slot in range(4)]

    def tensor(eng):
        for oc in range(NOC):
            wo = wob[oc % 2]
            eng.wait_ge(sve.h, ev[("ocv", oc)])
            for th in range(4):
                i = oc * 4 + th
                p = i % 2
                if i >= 2:
                    eng.wait_ge(sve.h, ev[("ox2", i - 2)])
                for j, dt in enumerate(DTSEQ):
                    if i == 0 and j % 4 == 0 and C["emit"]:
                        hl_ = j // 4
                        if hl_ == 3:
                            g_, v_ = ev[("oldg", 3, 1)]
                            eng.wait_ge(solg[g_].h, v_)
                        else:
                            g_, v_ = ev[("oldg", hl_, None)]
                            eng.wait_ge(solg[g_].h, v_)
                    mmi = eng.matmul(ps[p][:, :], wo[:, dt * 128:(dt + 1) * 128],
                                     o_sb[:, dt * T + th * TT: dt * T + (th + 1) * TT],
                                     start=(j == 0), stop=(j == NDT - 1))
                ev[("omm", oc, th)] = smm.inc(mmi)

    return {"sync": sync, "scalar": scalar, "vector": vector,
            "gpsimd": gpsimd, "tensor": tensor}


def phase_final(S_, C, ev):
    def gpsimd(eng):
        for p in range(2):
            eng.wait_ge(S_[f"f_o{p}"].h, S_[f"f_o{p}"].n)

    return {"gpsimd": gpsimd}


def make_phases(S_, C, ev):
    ph = [phase_const(S_, C, ev), "BARRIER"]
    if "ffn1" in PHASES:
        ph += [phase_ffn(S_, C, ev, C["x_fm"],
                         C["out_fm"] if C["dev_out"] == "x1" else C["x1d"], "f1",
                         do_conv=True),
               "BARRIER"]
    if "qkv" in PHASES:
        ph += [phase_qkv(S_, C, ev), "BARRIER"]
    if "attn" in PHASES:
        ph += [phase_attn(S_, C, ev), "BARRIER"]
    if "oproj" in PHASES:
        ph += [phase_oproj(S_, C, ev), "BARRIER"]
    if "ffn2" in PHASES:
        ph += [phase_ffn(S_, C, ev, C["x2d"], C["out_fm"], "f2"), "BARRIER"]
    ph.append(phase_final(S_, C, ev))
    return ph


def build(dev_out=None):
    nc = bass.Bass("TRN2", target_bir_lowering=False, debug=False)

    C = {"dev_out": dev_out}
    C["x_fm"] = nc.dram_tensor("x_fm", [D, T], F32, kind="ExternalInput").ap()
    C["wg8"] = nc.dram_tensor("wg8", [NIC * 128, D], I8, kind="ExternalInput").ap()
    C["wu8"] = nc.dram_tensor("wu8", [NIC * 128, D], I8, kind="ExternalInput").ap()
    C["wd8"] = nc.dram_tensor("wd8", [NOC * 128, INTER], I8, kind="ExternalInput").ap()
    C["wpT"] = nc.dram_tensor("wpT", [48 * 128, D], F32, kind="ExternalInput").ap()
    C["woT"] = nc.dram_tensor("woT", [16 * 128, D], F32, kind="ExternalInput").ap()
    sg_in = nc.dram_tensor("sg_in", [128, NIC], F32, kind="ExternalInput").ap()
    su_in = nc.dram_tensor("su_in", [128, NIC], F32, kind="ExternalInput").ap()
    sd_in = nc.dram_tensor("sd_in", [128, NOC], F32, kind="ExternalInput").ap()
    ones_in = nc.dram_tensor("ones_in", [128, 128], BF16, kind="ExternalInput").ap()
    ident_in = nc.dram_tensor("ident_in", [128, 128], BF16, kind="ExternalInput").ap()
    mask_in = nc.dram_tensor("mask_in", [128, 512], F32, kind="ExternalInput").ap()
    C["out_fm"] = nc.dram_tensor("out_fm", [D, T], F32, kind="ExternalOutput").ap()

    C["wg_bf"] = nc.dram_tensor("wg_bf", [NIC * 128, D], BF16, kind="Internal").ap()
    C["wu_bf"] = nc.dram_tensor("wu_bf", [NIC * 128, D], BF16, kind="Internal").ap()
    C["wd_bf"] = nc.dram_tensor("wd_bf", [NOC * 128, INTER], BF16, kind="Internal").ap()
    C["x1d"] = nc.dram_tensor("x1d", [D, T], F32, kind="Internal").ap()
    C["x2d"] = nc.dram_tensor("x2d", [D, T], F32, kind="Internal").ap()
    C["qkv_con"] = nc.dram_tensor("qkv_con", [3 * D, T], BF16, kind="Internal")
    C["qkv_gat"] = nc.dram_tensor("qkv_gat", [4 * 3 * D, T], BF16, kind="Internal")
    C["o_con"] = nc.dram_tensor("o_con", [NH_LOC * 128, S], BF16, kind="Internal")
    C["o_gat"] = nc.dram_tensor("o_gat", [4 * NH_LOC * 128, S], BF16, kind="Internal")

    from contextlib import ExitStack
    with ExitStack() as ctx:
        E = ctx.enter_context

        C["ones_sb"] = E(nc.sbuf_tensor("ones_sb", [128, 128], BF16))
        C["ident_sb"] = E(nc.sbuf_tensor("ident_sb", [128, 128], BF16))
        C["mask_sb"] = E(nc.sbuf_tensor("mask_sb", [128, 512], F32))
        C["sg_sb"] = E(nc.sbuf_tensor("sg_sb", [128, NIC], F32))
        C["su_sb"] = E(nc.sbuf_tensor("su_sb", [128, NIC], F32))
        C["sdh_sb"] = E(nc.sbuf_tensor("sdh_sb", [128, NOC], F32))
        C["const_pairs"] = (
            (ones_in, C["ones_sb"]), (ident_in, C["ident_sb"]),
            (mask_in, C["mask_sb"]), (sg_in, C["sg_sb"]), (su_in, C["su_sb"]),
            (sd_in, C["sdh_sb"]),
        )

        ARENA_B = 204 * 1024
        arena_t = E(nc.sbuf_tensor("arena", [128, ARENA_B], I8))
        arena = arena_t.ap()

        def carve(off, nbytes, dtype, cols):
            ap = arena[:, off[0]:off[0] + nbytes].bitcast(dtype)
            assert ap.shape[1] == cols, (ap.shape, cols)
            off[0] += (nbytes + 31) & ~31
            assert off[0] <= ARENA_B, off[0]
            return ap

        o = [0]
        C["cv_g8"] = [carve(o, 2048, I8, 2048) for _ in range(2)]
        C["cv_gb"] = [carve(o, 4096, BF16, 2048) for _ in range(2)]
        C["cv_u8"] = [carve(o, 2048, I8, 2048) for _ in range(2)]
        C["cv_ub"] = [carve(o, 4096, BF16, 2048) for _ in range(2)]
        C["cv_d8"] = [carve(o, 8192, I8, 8192) for _ in range(2)]
        C["cv_db"] = [carve(o, 16384, BF16, 8192) for _ in range(2)]

        o = [0]
        C["ffn_xf"] = carve(o, NDT * TT * 4, F32, NDT * TT)
        C["ffn_xn2"] = [carve(o, NDT * TT * 2, BF16, NDT * TT) for _ in range(2)]
        C["ffn_xr"] = [carve(o, TT * 4, F32, TT) for _ in range(2)]
        C["ffn_h"] = carve(o, NIC * TT * 2, BF16, NIC * TT)
        C["ffn_wg"] = [carve(o, 4096, BF16, 2048) for _ in range(2)]
        C["ffn_wu"] = [carve(o, 4096, BF16, 2048) for _ in range(2)]
        C["ffn_wd"] = [carve(o, 16384, BF16, 8192) for _ in range(2)]
        C["ffn_g"] = [carve(o, TT * 4, F32, TT) for _ in range(2)]
        C["ffn_sq"] = [carve(o, TT * 2, BF16, TT) for _ in range(2)]
        C["ffn_xb"] = [carve(o, TT * 2, BF16, TT) for _ in range(2)]
        C["ffn_lnt"] = carve(o, TT * 4, F32, TT)
        C["ffn_m"] = carve(o, TT * 4, F32, TT)
        C["ffn_v"] = carve(o, TT * 4, F32, TT)
        C["ffn_A"] = carve(o, TT * 4, F32, TT)
        C["ffn_B"] = carve(o, TT * 4, F32, TT)
        C["ffn_x1"] = [carve(o, TT * 4, F32, TT) for _ in range(2)]

        o = [0]
        C["q_xn"] = carve(o, NDT * T * 2, BF16, NDT * T)
        C["q_xc"] = [carve(o, TT * 4, F32, TT) for _ in range(2)]
        C["q_sq"] = [carve(o, TT * 2, BF16, TT) for _ in range(2)]
        C["q_xb"] = [carve(o, TT * 2, BF16, TT) for _ in range(2)]
        C["q_m"] = carve(o, TT * 4, F32, TT)
        C["q_v"] = carve(o, TT * 4, F32, TT)
        C["q_A"] = carve(o, TT * 4, F32, TT)
        C["q_B"] = carve(o, TT * 4, F32, TT)
        C["q_lnt"] = carve(o, TT * 4, F32, TT)
        C["q_dum"] = carve(o, 4, F32, 1)
        C["q_wpf"] = [carve(o, 8192, F32, 2048) for _ in range(2)]
        C["q_wpb"] = [carve(o, 4096, BF16, 2048) for _ in range(6)]
        C["q_ob"] = [carve(o, TT * 2, BF16, TT) for _ in range(2)]

        o = [0]
        C["at_q"] = [carve(o, S * 2, BF16, S) for _ in range(2)]
        C["at_k"] = [carve(o, S * 2, BF16, S) for _ in range(2)]
        C["at_vf"] = [carve(o, S * 2, BF16, S) for _ in range(2)]
        C["at_vt"] = [carve(o, S * 2, BF16, S) for _ in range(2)]
        C["at_o"] = carve(o, NH_LOC * S * 2, BF16, NH_LOC * S)
        C["at_ps"] = [carve(o, 512 * 2, BF16, 512) for _ in range(4)]
        C["at_scr"] = carve(o, 512 * 4, F32, 512)
        C["at_rs"] = [carve(o, 256 * 4, F32, 256) for _ in range(2)]
        C["at_dum"] = carve(o, 4, F32, 1)

        o = [0]
        C["op_o"] = carve(o, NDT * T * 2, BF16, NDT * T)
        C["op_wf"] = [carve(o, 8192, F32, 2048) for _ in range(2)]
        C["op_wb"] = [carve(o, 4096, BF16, 2048) for _ in range(2)]
        C["op_x1"] = [carve(o, TT * 4, F32, TT) for _ in range(2)]
        C["op_x2"] = [carve(o, TT * 4, F32, TT) for _ in range(2)]

        C["PSW"] = [E(nc.psum_tensor(f"psw{i}", [128, 1024], F32)) for i in range(2)]
        C["PS"] = [C["PSW"][0][:, 0:512], C["PSW"][0][:, 512:1024],
                   C["PSW"][1][:, 0:512], C["PSW"][1][:, 512:1024]] + \
                  [E(nc.psum_tensor(f"ps{i}", [128, 512], F32)) for i in range(4, 8)]

        ev = EvDict()
        ev.strict = False
        C["emit"] = False
        S0 = {k: Sem(None) for k in SEM_NAMES}
        mock = _MockEng()
        snapshots = []
        phases0 = make_phases(S0, C, ev)
        for ph in phases0:
            if ph == "BARRIER":
                snapshots.append({k: s.n for k, s in S0.items()})
                continue
            for name in ENGS:
                if name in ph:
                    ph[name](mock)
        final_counts = {k: s.n for k, s in S0.items()}

        ev.strict = True
        C["emit"] = True
        S1 = {k: Sem(ctx.enter_context(nc.semaphore(k))) for k in SEM_NAMES}
        phases1 = make_phases(S1, C, ev)
        bi = [0]

        def emit_engine(eng, name):
            b = 0
            for ph in phases1:
                if ph == "BARRIER":
                    snap = snapshots[b]
                    b += 1
                    for k, v in snap.items():
                        if v > 0 and not k.startswith(("cc1g", "cc2g")):
                            eng.wait_ge(S1[k].h, v)
                    continue
                if name in ph:
                    ph[name](eng)

        with nc.Block() as block:
            block.sync(lambda eng: emit_engine(eng, "sync"))
            block.tensor(lambda eng: emit_engine(eng, "tensor"))
            block.vector(lambda eng: emit_engine(eng, "vector"))
            block.scalar(lambda eng: emit_engine(eng, "scalar"))
            block.gpsimd(lambda eng: emit_engine(eng, "gpsimd"))

        for k in SEM_NAMES:
            assert final_counts[k] == S1[k].n, (k, final_counts[k], S1[k].n)

    return nc


def _prep(inputs):
    x = np.ascontiguousarray(np.asarray(inputs["L0_attn_x"], f32))
    wg = np.asarray(inputs["L0_ffn_w_gate"], np.int8)
    wu = np.asarray(inputs["L0_ffn_w_up"], np.int8)
    wd = np.asarray(inputs["L0_ffn_w_down"], np.int8)
    wp = np.asarray(inputs["L0_attn_w_project"], f32)
    wo = np.asarray(inputs["L0_attn_w_out"], f32)
    sg = np.asarray(inputs["L0_ffn_s_gate"], f32)
    su = np.asarray(inputs["L0_ffn_s_up"], f32)
    sd = np.asarray(inputs["L0_ffn_s_down"], f32)

    wg8 = np.ascontiguousarray(
        wg.reshape(16, 128, 64, 128).transpose(2, 1, 0, 3).reshape(64 * 128, 2048))
    wu8 = np.ascontiguousarray(
        wu.reshape(16, 128, 64, 128).transpose(2, 1, 0, 3).reshape(64 * 128, 2048))
    wd8 = np.ascontiguousarray(
        wd.reshape(64, 128, 16, 128).transpose(2, 1, 0, 3).reshape(16 * 128, 8192))
    wpT = np.ascontiguousarray(
        wp.reshape(48, 128, 16, 128).transpose(0, 3, 2, 1).reshape(48 * 128, 2048))
    woT = np.ascontiguousarray(
        wo.reshape(16, 128, 16, 128).transpose(0, 3, 2, 1).reshape(16 * 128, 2048))
    sg2 = np.ascontiguousarray(sg.reshape(64, 128).T)
    su2 = np.ascontiguousarray(su.reshape(64, 128).T)
    sd2 = np.ascontiguousarray(sd.reshape(16, 128).T)
    ones = np.ones((128, 128), dtype=BF)
    ident = np.eye(128, dtype=BF)
    kk = np.arange(128)[:, None]
    qq = np.arange(256)[None, :]
    mA = np.where(kk > qq, f32(-1e9), f32(0.0))
    mB = np.where(kk + 128 > qq, f32(-1e9), f32(0.0))
    mask = np.concatenate([mA, mB], axis=1).astype(f32)

    common = dict(wg8=wg8, wu8=wu8, wd8=wd8, wpT=wpT, woT=woT,
                  sg_in=sg2, su_in=su2, sd_in=sd2,
                  ones_in=ones, ident_in=ident, mask_in=mask)
    in_maps = []
    for c in range(8):
        b, m = c // 4, c % 4
        xc = np.ascontiguousarray(x[b, m * T:(m + 1) * T, :].T)
        in_maps.append(dict(common, x_fm=xc))
    return in_maps


def kernel(**inputs):
    in_maps = _prep(inputs)
    nc = build()
    res = run_bass_kernel_spmd(nc, in_maps, core_ids=list(range(8)))
    out = np.empty((2, S, D), f32)
    for c in range(8):
        b, m = c // 4, c % 4
        out[b, m * T:(m + 1) * T, :] = res.results[c]["out_fm"].T
    return out
